# version 01
# baseline (speedup 1.0000x reference)
import numpy as np
import ml_dtypes
import concourse.bass as bass
import concourse.mybir as mybir
from concourse.bass_utils import run_bass_kernel_spmd

F32 = mybir.dt.float32
BF16 = mybir.dt.bfloat16
AF = mybir.ActivationFunctionType
OP = mybir.AluOpType
AX = mybir.AxisListType

P = 128
D = 2048
KD = 16
WARM = 512
TMAIN = 2048
TEXT = WARM + TMAIN
NCH = TEXT // P
NCW = WARM // P
NCM = TMAIN // P
DIN = 3072
JIN = DIN // P
NH = 48
DH = 64
DS = 128
NPROJ = 6448
NE = 8
FE = 1408
FS = 2816
FDOWN = NE * FE + FS
NW3 = 2 * NE * FE + 2 * FS
EPS = 1e-5

ENGINES = ("tensor", "vector", "scalar", "gpsimd", "sync")


class Prog:

    def __init__(self, nc):
        self.nc = nc
        self.semh = {}
        self.semn = {}
        self.ops = {e: [] for e in ENGINES}

    def sem(self, name):
        if name not in self.semh:
            self.semh[name] = self.nc.alloc_semaphore(name)
            self.semn[name] = 0
        return name

    def n(self, name):
        return self.semn[self.sem(name)]

    def op(self, eng, fn, waits=(), inc=None, inck=1):
        w = tuple((self.sem(s), int(v)) for s, v in waits if v > 0)
        if inc is not None:
            self.sem(inc)
            self.semn[inc] += inck
        self.ops[eng].append((fn, w, inc, inck))
        return self.semn[inc] if inc is not None else None

    def emit(self, blk):
        for eng in ENGINES:
            oplist = self.ops[eng]
            if not oplist:
                continue

            @getattr(blk, eng)
            def _(e, oplist=oplist):
                seen = {}
                for fn, waits, inc, inck in oplist:
                    for s, v in waits:
                        if seen.get(s, 0) >= v:
                            continue
                        e.wait_ge(self.semh[s], v)
                        seen[s] = v
                    ins = fn(e)
                    if inc is not None:
                        if ins is None:
                            ins = e.nop()
                        ins.then_inc(self.semh[inc], inck)


def ceil_div(a, b):
    return (a + b - 1) // b


class Kernel:
    def __init__(self, dbg=None):
        self.dbg = set(dbg or ())
        self.dbg_names = []
        nc = bass.Bass("TRN2", target_bir_lowering=False)
        self.nc = nc
        self.pr = Prog(nc)

        dr = nc.dram_tensor
        self.x_h = dr("x_ext", [TEXT, D], F32, kind="ExternalInput").ap()
        self.w1_h = dr("w1t", [D, NPROJ], F32, kind="ExternalInput").ap()
        self.w3_h = dr("w3t", [D, NW3], F32, kind="ExternalInput").ap()
        self.wd_h = dr("wdt", [FDOWN, D], F32, kind="ExternalInput").ap()
        self.wo_h = dr("wot", [DIN, D], F32, kind="ExternalInput").ap()
        self.gw_h = dr("gwt", [D, NE], F32, kind="ExternalInput").ap()
        self.normw_h = dr("normw_kp", [P, KD], F32, kind="ExternalInput").ap()
        self.innerw_h = dr("innerw_kp", [P, JIN], F32, kind="ExternalInput").ap()
        self.convw_h = dr("convw_fm", [P, JIN, 4], F32, kind="ExternalInput").ap()
        self.convb_h = dr("convb_fm", [P, JIN], F32, kind="ExternalInput").ap()
        self.cs_h = dr("cs_fm", [P, JIN, 3], F32, kind="ExternalInput").ap()
        self.h0_h = dr("h0_fm", [DS, NH * DH], F32, kind="ExternalInput").ap()
        self.abd_h = dr("abd", [NH, 4], F32, kind="ExternalInput").ap()
        self.m_h = dr("mvec", [P, 2], F32, kind="ExternalInput").ap()

        self.out_h = dr("out", [TMAIN, D], F32, kind="ExternalOutput").ap()

        self.zd = dr("z_scr", [TMAIN, DIN], BF16).ap()
        self.xcd = dr("xc_scr", [JIN, P, TEXT], BF16).ap()
        self.ad = dr("a_scr", [FDOWN, TMAIN], BF16).ap()
        self.ygd = dr("yg_scr", [JIN, P, TMAIN], BF16).ap()
        self.ld = dr("l_scr", [NH, TEXT], F32).ap()
        self.wmd = dr("wm_scr", [NE, TMAIN], F32).ap()
        self.dxd = dr("dx_scr", [NCH, NH], F32).ap()
        self.rsd = dr("rs_scr", [NCH, P], F32).ap()

        A = {}

        def sb(name, shape, dt):
            t = nc.alloc_sbuf_tensor(name, shape, dt)
            A[name] = t
            return t

        self.identb = sb("identb", [P, P], BF16)
        self.identf = sb("identf", [P, P], F32)
        self.onesf = sb("onesf", [1, P], F32)
        self.cc = sb("constcol", [P, 4], F32)
        self.normw = sb("normw", [P, KD], F32)
        self.innerw = sb("innerw", [P, JIN], F32)
        self.convw = sb("convw", [P, JIN, 4], F32)
        self.convb = sb("convb", [P, JIN], F32)
        self.csfm = sb("csfm", [P, JIN, 3], F32)
        self.abd = sb("abd_sb", [NH, 4], F32)
        self.negA = sb("negA", [NH, 1], F32)
        self.Dbc = sb("Dbc", [P, NH], F32)
        self.mv = sb("mv", [P, 2], F32)
        self.gwf = sb("gwf", [P, KD, NE], F32)
        self.gwhi = sb("gwhi", [P, KD, NE], BF16)
        self.gwlo = sb("gwlo", [P, KD, NE], BF16)
        self.mrstd = sb("mrstd", [P, 2 * NCH], F32)
        self.rstok = sb("rstok", [P, NCH], F32)
        self.scores = sb("scores", [P, NCM, NE], F32)
        self.wmat = sb("wmat", [P, NCM, NE], F32)

        self.pbank = [nc.alloc_psum_tensor(f"bank{i}", [P, 512], F32) for i in range(8)]

        BASE = 82 * 1024
        self.man_off = BASE
        self.man_map = {}

        self.xnT = self.man("xnT", [P, KD, TEXT], BF16, 0)

        self.built = False

    def man(self, name, shape, dt, off_rel):
        nbytes = int(np.prod(shape[1:])) * mybir.dt.size(dt)
        t = self.nc.alloc_sbuf_tensor_at(name, shape, dt, offset=self.man_off + off_rel)
        self.man_map[name] = (off_rel, nbytes)
        return t

    def dump(self, name, ap, shape=None, dt=None):
        if name not in self.dbg:
            return
        nc, pr = self.nc, self.pr
        shape = shape or list(ap.shape)
        dt = dt or ap.dtype
        o = nc.dram_tensor("dbg_" + name, shape, dt, kind="ExternalOutput").ap()
        self.dbg_names.append("dbg_" + name)
        waits = [(s, pr.n(s)) for s in list(pr.semn)]
        pr.op("sync", lambda e, ap=ap, o=o: e.dma_start(out=o, in_=ap),
              waits=waits, inc="dS", inck=16)

    def barrier(self, tag):
        pr = self.pr
        waits = [(s, pr.n(s)) for s in list(pr.semn)]
        for eng in ENGINES:
            pr.op(eng, lambda e: e.nop(), waits=waits, inc="bar")

    def phase_consts(self):
        pr, nc = self.pr, self.nc

        for dst, src in [
            (self.normw[:], self.normw_h), (self.innerw[:], self.innerw_h),
            (self.convw[:], self.convw_h), (self.convb[:], self.convb_h),
            (self.csfm[:], self.cs_h), (self.abd[:], self.abd_h),
            (self.mv[:], self.m_h),
            (self.gwf[:], self.gw_h.rearrange("(k p) e -> p k e", p=P)),
        ]:
            pr.op("sync", lambda e, dst=dst, src=src: e.dma_start(out=dst, in_=src),
                  inc="dW", inck=16)

        def idents(e):
            e.memset(self.identb[:], 0.0)
            e.memset(self.identf[:], 0.0)
            e.memset(self.onesf[:], 1.0)
            e.memset(self.cc[:, 0:1], EPS)
            e.memset(self.cc[:, 1:2], 1.0)
            e.memset(self.cc[:, 2:3], 0.0)
            e.affine_select(out=self.identb[:], in_=self.identb[:],
                            compare_op=OP.not_equal, fill=1.0, base=0,
                            pattern=[[-1, P]], channel_multiplier=1)
            return e.affine_select(out=self.identf[:], in_=self.identf[:],
                                   compare_op=OP.not_equal, fill=1.0, base=0,
                                   pattern=[[-1, P]], channel_multiplier=1)

        pr.op("gpsimd", idents, inc="po")

        def gwsplit(e):
            e.tensor_copy(self.gwhi[:], self.gwf[:])
            return e.tensor_tensor(self.gwlo[:], self.gwf[:], self.gwhi[:], OP.subtract)

        pr.op("gpsimd", gwsplit, waits=[("dW", pr.n("dW"))], inc="po")

        pr.op("scalar", lambda e: e.activation(self.negA[:], self.abd[:, 0:1], AF.Exp),
              waits=[("dW", pr.n("dW"))], inc="ac")
        pr.op("vector", lambda e: e.tensor_scalar_mul(self.negA[:], self.negA[:], -1.0),
              waits=[("ac", pr.n("ac"))], inc="dv")

        self.psK = self.pbank[0]
        drow = nc.alloc_sbuf_tensor("drow", [1, NH], F32)
        pr.op("tensor", lambda e: e.matmul(
            self.psK[0:1, 0:NH], self.abd[:, 2:3], self.identf[0:NH, 0:NH],
            is_transpose=True, start=True, stop=True),
            waits=[("dW", pr.n("dW")), ("po", pr.n("po"))], inc="pe")
        pr.op("vector", lambda e: e.tensor_copy(drow[:], self.psK[0:1, 0:NH]),
              waits=[("pe", pr.n("pe"))], inc="dv")
        pr.op("tensor", lambda e: e.matmul(
            self.psK[:, 64:64 + NH], self.onesf[:, :], drow[:], start=True, stop=True),
            waits=[("dv", pr.n("dv"))], inc="pe")
        pr.op("vector", lambda e: e.tensor_copy(self.Dbc[:], self.psK[:, 64:64 + NH]),
              waits=[("pe", pr.n("pe"))], inc="dv")

    def phase_p0(self):
        pr, nc = self.pr, self.nc
        xld = [self.man("xld0", [P, D], F32, 80 * 1024),
               self.man("xld1", [P, D], F32, 88 * 1024)]
        tf = self.man("tf", [P, D], F32, 96 * 1024)
        hi = self.man("hi", [P, D], BF16, 104 * 1024)
        lo = self.man("lo", [P, D], BF16, 108 * 1024)
        loT = self.man("loT", [P, KD * P], BF16, 112 * 1024)
        sqs = self.man("sqs", [P, D], BF16, 116 * 1024)
        srow = nc.alloc_sbuf_tensor("srow", [P, 1], F32)
        ssq = nc.alloc_sbuf_tensor("ssq", [P, 1], F32)
        t1 = nc.alloc_sbuf_tensor("t1_", [P, 1], F32)
        t2 = nc.alloc_sbuf_tensor("t2_", [P, 1], F32)
        lgsb = nc.alloc_sbuf_tensor("lgsb", [NE, P], F32)
        psTa = self.pbank[1][:].bitcast(BF16)
        psTb = self.pbank[2][:].bitcast(BF16)
        psL = self.pbank[3][0:NE, 0:P]

        xfree = [0, 0]
        xfree_ac = [0, 0]
        tpfree = [0, 0]
        self.pe_hilo = 0

        import os
        nch_lim = int(os.environ.get('P2_CHUNKS', NCH))
        MLVL = int(os.environ.get('P2_MLVL', 9))
        for c in range(nch_lim):
            par = c % 2
            main = c >= NCW
            cm = c - NCW

            pr.op("sync", lambda e, c=c, par=par: e.dma_start(
                out=xld[par][:], in_=self.x_h[c * P:(c + 1) * P, :]),
                waits=[("dv", xfree[par]), ("ac", xfree_ac[par])], inc="dX", inck=16)
            dx_here = pr.n("dX")

            pr.op("vector", lambda e, par=par: e.reduce_sum(srow[:], xld[par][:], axis=AX.X),
                  waits=[("dX", dx_here)], inc="dv")
            pr.op("scalar", lambda e, par=par: e.activation(
                sqs[:], xld[par][:], AF.Square, accum_out=ssq[:]),
                waits=[("dX", dx_here)], inc="ac")
            xfree_ac[par] = pr.n("ac")

            mcol = self.mrstd[:, 2 * c:2 * c + 1]
            rcol = self.mrstd[:, 2 * c + 1:2 * c + 2]
            pr.op("vector", lambda e, mcol=mcol: e.tensor_scalar_mul(mcol, srow[:], 1.0 / D),
                  waits=[("ac", pr.n("ac")), ("dv", pr.n("dv"))], inc="dv")
            pr.op("vector", lambda e: e.tensor_scalar_mul(t1[:], ssq[:], 1.0 / D),
                  waits=[("dv", pr.n("dv"))], inc="dv")
            pr.op("vector", lambda e, mcol=mcol: e.tensor_tensor(t2[:], mcol, mcol, OP.mult),
                  waits=[("dv", pr.n("dv"))], inc="dv")
            pr.op("vector", lambda e: e.tensor_sub(t1[:], t1[:], t2[:]),
                  waits=[("dv", pr.n("dv"))], inc="dv")
            pr.op("scalar", lambda e: e.activation(t2[:], t1[:], AF.Ln, bias=self.cc[:, 0:1]),
                  waits=[("dv", pr.n("dv"))], inc="ac")
            pr.op("scalar", lambda e, rcol=rcol: e.activation(rcol, t2[:], AF.Exp, scale=-0.5),
                  waits=[("ac", pr.n("ac"))], inc="ac")
            pr.op("vector", lambda e, rcol=rcol: e.tensor_tensor(t2[:], rcol, rcol, OP.mult),
                  waits=[("ac", pr.n("ac"))], inc="dv")
            pr.op("vector", lambda e: e.tensor_mul(t2[:], t2[:], t1[:]),
                  waits=[("dv", pr.n("dv"))], inc="dv")
            pr.op("scalar", lambda e: e.activation(t2[:], t2[:], AF.Ln, bias=self.cc[:, 0:1]),
                  waits=[("dv", pr.n("dv"))], inc="ac")
            pr.op("scalar", lambda e, c=c: e.activation(
                self.rstok[:, c:c + 1], t2[:], AF.Exp, scale=-0.5),
                waits=[("ac", pr.n("ac"))], inc="ac")

            def xn_ops(e, par=par, main=main, mcol=mcol, rcol=rcol):
                e.tensor_scalar(tf[:], xld[par][:], mcol, rcol, OP.subtract, OP.mult)
                e.tensor_copy(hi[:], tf[:])
                if main:
                    return e.tensor_tensor(lo[:], tf[:], hi[:], OP.subtract)
                return e.engine_nop()
            pr.op("vector", xn_ops,
                  waits=[("ac", pr.n("ac")), ("pe", self.pe_hilo)], inc="dv")
            xfree[par] = pr.n("dv")

            waves = [(hi, 0, psTa), (hi, 8, psTb)] + ([(lo, 0, psTa), (lo, 8, psTb)] if main else [])
            for wi, (src, k0, bank) in enumerate(waves):
                bfree = tpfree[0] if bank is psTa else tpfree[1]

                def tp(e, src=src, k0=k0, bank=bank):
                    last = None
                    for kk in range(8):
                        k = k0 + kk
                        last = e.matmul(bank[:, kk * P:(kk + 1) * P],
                                        src[:, k * P:(k + 1) * P], self.identb[:],
                                        is_transpose=True, start=(kk == 0), stop=(kk == 7))
                    return last
                pr.op("tensor", tp, waits=[("dv", pr.n("dv")), ("ac", bfree)], inc="pe")
                pe_here = pr.n("pe")

                def ev(e, src=src, k0=k0, bank=bank, c=c):
                    last = None
                    for kk in range(8):
                        k = k0 + kk
                        dst = (self.xnT[:, k, c * P:(c + 1) * P] if src is hi
                               else loT[:, k * P:(k + 1) * P])
                        last = e.copy(dst, bank[:, kk * P:(kk + 1) * P])
                    return last
                pr.op("scalar", ev, waits=[("pe", pe_here)], inc="ac")
                if bank is psTa:
                    tpfree[0] = pr.n("ac")
                else:
                    tpfree[1] = pr.n("ac")

            self.pe_hilo = pr.n("pe")
            if main:
                def logit_mms(e, c=c):
                    last = None
                    first = True
                    for (gwt, xsrc) in ((self.gwhi, "hi"), (self.gwlo, "hi"), (self.gwhi, "lo")):
                        for k in range(KD):
                            rhs = (self.xnT[:, k, c * P:(c + 1) * P] if xsrc == "hi"
                                   else loT[:, k * P:(k + 1) * P])
                            last = e.matmul(psL[:], gwt[:, k, :], rhs,
                                            start=first, stop=(not first and k == KD - 1
                                                               and gwt is self.gwhi and xsrc == "lo"))
                            first = False
                    return last
                pr.op("tensor", logit_mms, waits=[("ac", pr.n("ac"))], inc="pe")
                self.pe_hilo = pr.n("pe")
                pr.op("scalar", lambda e: e.copy(lgsb[:], psL[:]),
                      waits=[("pe", pr.n("pe"))], inc="ac")
                pr.op("tensor", lambda e: e.matmul(
                    self.psK[:, 128:128 + NE], lgsb[:], self.identf[0:NE, 0:NE],
                    is_transpose=True, start=True, stop=True),
                    waits=[("ac", pr.n("ac")), ("dv", pr.n("dv"))], inc="pe")
                pr.op("vector", lambda e, cm=cm: e.tensor_copy(
                    self.scores[:, cm, :], self.psK[:, 128:128 + NE]),
                    waits=[("pe", pr.n("pe"))], inc="dv")

        pr.op("tensor", lambda e: e.matmul(
            self.psK[0:NCH, 256:256 + P], self.rstok[:], self.identf[:],
            is_transpose=True, start=True, stop=True),
            waits=[("ac", pr.n("ac"))], inc="pe")
        rs_t = nc.alloc_sbuf_tensor("rs_t", [NCH, P], F32)
        pr.op("vector", lambda e: e.tensor_copy(rs_t[:], self.psK[0:NCH, 256:256 + P]),
              waits=[("pe", pr.n("pe"))], inc="dv")
        pr.op("sync", lambda e: e.dma_start(out=self.rsd, in_=rs_t[:]),
              waits=[("dv", pr.n("dv"))], inc="dS", inck=16)

        self.dump("mrstd", self.mrstd[:])
        self.dump("rstok", self.rstok[:])
        self.dump("xnT", self.xnT[:])
        self.dump("scores_raw", self.scores[:])

        mx = nc.alloc_sbuf_tensor("mx", [P, NCM], F32)
        ex = nc.alloc_sbuf_tensor("ex", [P, NCM, NE], F32)
        sm = nc.alloc_sbuf_tensor("sm", [P, NCM], F32)
        gsc = nc.alloc_sbuf_tensor("gsc", [P, NCM, 4], F32)
        cmp4 = nc.alloc_sbuf_tensor("cmp4", [P, NCM, 4, 3], F32)
        cmp8 = nc.alloc_sbuf_tensor("cmp8", [P, NCM, NE, 7], F32)
        rk4 = nc.alloc_sbuf_tensor("rk4", [P, NCM, 4], F32)
        rk8 = nc.alloc_sbuf_tensor("rk8", [P, NCM, NE], F32)
        msc = nc.alloc_sbuf_tensor("msc", [P, NCM, NE], F32)

        S = self.scores
        dvw = pr.n("dv")
        pr.op("vector", lambda e: e.reduce_max(mx[:, :, None], S[:], axis=AX.X),
              waits=[("dv", dvw)], inc="dv")
        pr.op("vector", lambda e: e.tensor_tensor(
            ex[:], S[:], mx[:, :, None].broadcast_to([P, NCM, NE]), OP.subtract),
            waits=[("dv", pr.n("dv"))], inc="dv")
        pr.op("scalar", lambda e: e.activation(ex[:], ex[:], AF.Exp),
              waits=[("dv", pr.n("dv"))], inc="ac")
        pr.op("vector", lambda e: e.reduce_sum(sm[:, :, None], ex[:], axis=AX.X),
              waits=[("ac", pr.n("ac"))], inc="dv")
        pr.op("vector", lambda e: e.reciprocal(sm[:], sm[:]),
              waits=[("dv", pr.n("dv"))], inc="dv")
        pr.op("vector", lambda e: e.tensor_tensor(
            S[:], ex[:], sm[:, :, None].broadcast_to([P, NCM, NE]), OP.mult),
            waits=[("dv", pr.n("dv"))], inc="dv")
        pr.op("vector", lambda e: e.tensor_tensor(
            gsc[:], S[:, :, 0:NE:2], S[:, :, 1:NE:2], OP.max),
            waits=[("dv", pr.n("dv"))], inc="dv")
        gready = pr.n("dv")
        slot = {g: 0 for g in range(4)}
        for g in range(4):
            for g2 in range(4):
                if g2 == g:
                    continue
                opx = OP.is_ge if g2 < g else OP.is_gt
                j = slot[g]; slot[g] += 1
                pr.op("vector", lambda e, g=g, g2=g2, j=j, opx=opx: e.tensor_tensor(
                    cmp4[:, :, g, j], S_g(gsc, g2), S_g(gsc, g), opx),
                    waits=[("dv", gready)], inc="dv")
        pr.op("vector", lambda e: e.reduce_sum(rk4[:, :, :, None], cmp4[:], axis=AX.X),
              waits=[("dv", pr.n("dv"))], inc="dv")
        pr.op("vector", lambda e: e.tensor_single_scalar(rk4[:], rk4[:], 2.0, OP.is_lt),
              waits=[("dv", pr.n("dv"))], inc="dv")
        pr.op("vector", lambda e: e.tensor_tensor(
            msc[:].rearrange("p c (g r) -> p c g r", r=2),
            S[:].rearrange("p c (g r) -> p c g r", r=2),
            rk4[:, :, :, None].broadcast_to([P, NCM, 4, 2]),
            OP.mult),
            waits=[("dv", pr.n("dv"))], inc="dv")
        mready = pr.n("dv")
        slot8 = {a: 0 for a in range(NE)}
        for a in range(NE):
            for b in range(NE):
                if a == b:
                    continue
                opx = OP.is_ge if b < a else OP.is_gt
                j = slot8[a]; slot8[a] += 1
                pr.op("vector", lambda e, a=a, b=b, j=j, opx=opx: e.tensor_tensor(
                    cmp8[:, :, a, j], msc[:, :, b], msc[:, :, a], opx),
                    waits=[("dv", mready)], inc="dv")
        pr.op("vector", lambda e: e.reduce_sum(rk8[:, :, :, None], cmp8[:], axis=AX.X),
              waits=[("dv", pr.n("dv"))], inc="dv")
        pr.op("vector", lambda e: e.tensor_single_scalar(rk8[:], rk8[:], 2.0, OP.is_lt),
              waits=[("dv", pr.n("dv"))], inc="dv")
        pr.op("vector", lambda e: e.tensor_tensor(self.wmat[:], msc[:], rk8[:], OP.mult),
              waits=[("dv", pr.n("dv"))], inc="dv")
        pr.op("tensor", lambda e: e.matmul(
            self.psK[:, 384:384 + P], self.wmat[:].rearrange("p c e -> p (c e)"),
            self.identf[:], is_transpose=True, start=True, stop=True),
            waits=[("dv", pr.n("dv"))], inc="pe")
        wmT = nc.alloc_sbuf_tensor("wmT", [P, P], F32)
        pr.op("vector", lambda e: e.tensor_copy(wmT[:], self.psK[:, 384:384 + P]),
              waits=[("pe", pr.n("pe"))], inc="dv")
        dvw2 = pr.n("dv")
        for iex in range(NE):
            pr.op("sync", lambda e, iex=iex: e.dma_start(
                out=self.wmd[iex, :].rearrange("(c t) -> c t", t=P),
                in_=wmT[iex::NE, :]),
                waits=[("dv", dvw2)], inc="dS", inck=16)

        self.dump("scores", self.scores[:])
        self.dump("wmat", self.wmat[:])

    def phase_p1(self):
        pr, nc = self.pr, self.nc
        snap = [(s, pr.n(s)) for s in list(pr.semn)]

        self.B_fm = nc.alloc_sbuf_tensor("B_fm", [P, TEXT], BF16)
        self.C_fm = nc.alloc_sbuf_tensor("C_fm", [P, TMAIN], BF16)
        self.dt_fm = nc.alloc_sbuf_tensor("dt_fm", [NH, TEXT], F32)

        pb = [b[:] for b in self.pbank]
        self.pb = pb

        rs_bc = nc.alloc_sbuf_tensor("rs_bc", [P, 5, 512], F32)
        rsrow = nc.alloc_sbuf_tensor("rsrow", [1, 512], F32)
        rsflat = self.rsd.rearrange("c p -> (c p)")
        for tg in range(5):
            pr.op("sync", lambda e, tg=tg: e.dma_start(
                out=rsrow[:], in_=rsflat[tg * 512:(tg + 1) * 512][None, :]),
                waits=snap if tg == 0 else [("pe", pr.n("pe"))], inc="dA", inck=16)
            pr.op("tensor", lambda e: e.matmul(
                pb[7][:], self.onesf[:], rsrow[:], start=True, stop=True),
                waits=[("dA", pr.n("dA"))], inc="pe")
            pr.op("vector", lambda e, tg=tg: e.tensor_copy(rs_bc[:, tg, :], pb[7][:]),
                  waits=[("pe", pr.n("pe"))], inc="dv")
        rsbc_ready = pr.n("dv")

        ZH = 256
        wzf = [self.man(f"wzf{i}", [P, KD, ZH], F32, 80 * 1024 + i * 16 * 1024) for i in range(2)]
        wzb = [self.man(f"wzb{i}", [P, KD, ZH], BF16, 112 * 1024 + i * 8 * 1024) for i in range(2)]
        zst = nc.alloc_sbuf_tensor("zst", [P, 8, 512], BF16)
        w1r = self.w1_h.rearrange("(k p) c -> p k c", p=P)

        zld_free = [snap, snap]
        zcast_free = [0, 0]
        zst_free = [0] * 8
        pb_free_dv = [rsbc_ready] * 8

        for jz in range(6):
            c0 = jz * 512
            for half in range(2):
                pr.op("sync", lambda e, c0=c0, half=half: e.dma_start(
                    out=wzf[half][:], in_=w1r[:, :, c0 + half * ZH:c0 + (half + 1) * ZH]),
                    waits=zld_free[half], inc="dW", inck=16)
                dw = pr.n("dW")
                pr.op("gpsimd", lambda e, half=half: e.tensor_tensor(
                    wzb[half][:], wzf[half][:],
                    self.normw[:, :, None].broadcast_to([P, KD, ZH]), OP.mult),
                    waits=[("dW", dw), ("pe", zcast_free[half])], inc="po")
                zld_free[half] = [("po", pr.n("po"))]
            po_ready = pr.n("po")

            for tcg in range(2):
                for half in range(2):
                    def zmm(e, tcg=tcg, half=half):
                        last = None
                        for k in range(KD):
                            for tc in range(8):
                                t0 = WARM + tcg * 1024 + tc * P
                                last = e.matmul(
                                    pb[tc][:, half * ZH:(half + 1) * ZH],
                                    self.xnT[:, k, t0:t0 + P],
                                    wzb[half][:, k, :],
                                    start=(half == 0 and k == 0),
                                    stop=(half == 1 and k == KD - 1))
                        return last
                    pr.op("tensor", zmm,
                          waits=[("po", po_ready)] + [("dv", max(pb_free_dv))],
                          inc="pe")
                zcast_free[0] = zcast_free[1] = pr.n("pe")
                pe_done = pr.n("pe")
                for tc in range(8):
                    cidx = NCW + tcg * 8 + tc
                    pr.op("vector", lambda e, tc=tc, cidx=cidx: e.tensor_scalar_mul(
                        zst[:, tc, :], pb[tc][:], self.rstok[:, cidx:cidx + 1]),
                        waits=[("pe", pe_done), ("dS", zst_free[tc])], inc="dv")
                    pb_free_dv[tc] = pr.n("dv")
                    pr.op("scalar", lambda e, tc=tc, tcg=tcg, jz=jz: e.dma_start(
                        out=self.zd[(tcg * 8 + tc) * P:(tcg * 8 + tc + 1) * P,
                                    jz * 512:(jz + 1) * 512],
                        in_=zst[:, tc, :]),
                        waits=[("dv", pr.n("dv"))], inc="dS", inck=16)
                    zst_free[tc] = pr.n("dS")

        wjf = [self.man(f"wjf{i}", [P, KD, P], F32, 80 * 1024 + i * 8 * 1024) for i in range(2)]
        wjb = [self.man(f"wjb{i}", [P, KD, P], BF16, 96 * 1024 + i * 4 * 1024) for i in range(2)]
        xcst = nc.alloc_sbuf_tensor("xcst", [P, 2, 512], BF16)
        z_done = [("pe", pr.n("pe")), ("po", pr.n("po"))]

        jobs = [("xc", 3072 + 128 * j, P, j) for j in range(JIN)]
        jobs += [("B", 6144, P, 0), ("C", 6272, P, 0), ("dt", 6400, NH, 0)]
        jld_free = [z_done, z_done]
        jcast_free = [0, 0]
        xc_free = [0, 0]
        fm_pb_free = [max(pb_free_dv)] * 5

        for ji, (kind, c0, w, j) in enumerate(jobs):
            par = ji % 2
            pr.op("sync", lambda e, c0=c0, w=w, par=par: e.dma_start(
                out=wjf[par][:, :, 0:w], in_=w1r[:, :, c0:c0 + w]),
                waits=jld_free[par], inc="dW", inck=16)
            dw = pr.n("dW")
            pr.op("gpsimd", lambda e, par=par, w=w: e.tensor_tensor(
                wjb[par][:, :, 0:w], wjf[par][:, :, 0:w],
                self.normw[:, :, None].broadcast_to([P, KD, w]), OP.mult),
                waits=[("dW", dw), ("pe", jcast_free[par])], inc="po")
            jld_free[par] = [("po", pr.n("po"))]
            po_ready = pr.n("po")

            tgs = [1, 2, 3, 4] if kind == "C" else [0, 1, 2, 3, 4]
            for tgi, tg in enumerate(tgs):
                def jmm(e, par=par, w=w, tg=tg, tgi=tgi):
                    last = None
                    for k in range(KD):
                        last = e.matmul(pb[tgi][0:w, :],
                                        wjb[par][:, k, 0:w],
                                        self.xnT[:, k, tg * 512:(tg + 1) * 512],
                                        start=(k == 0), stop=(k == KD - 1))
                    return last
                pr.op("tensor", jmm,
                      waits=[("po", po_ready), ("dv", fm_pb_free[tgi])], inc="pe")
            jcast_free[par] = pr.n("pe")
            pe_done = pr.n("pe")

            for tgi, tg in enumerate(tgs):
                t0 = tg * 512
                if kind == "xc":
                    spar = tgi % 2
                    pr.op("vector", lambda e, tgi=tgi, tg=tg, spar=spar: e.tensor_tensor(
                        xcst[:, spar, :], pb[tgi][:], rs_bc[:, tg, :], OP.mult),
                        waits=[("pe", pe_done), ("dS", xc_free[spar])], inc="dv")
                    fm_pb_free[tgi] = pr.n("dv")
                    pr.op("scalar", lambda e, j=j, t0=t0, spar=spar: e.dma_start(
                        out=self.xcd[j, :, t0:t0 + 512], in_=xcst[:, spar, :]),
                        waits=[("dv", pr.n("dv"))], inc="dS", inck=16)
                    xc_free[spar] = pr.n("dS")
                elif kind == "B":
                    pr.op("vector", lambda e, tgi=tgi, tg=tg, t0=t0: e.tensor_tensor(
                        self.B_fm[:, t0:t0 + 512], pb[tgi][:], rs_bc[:, tg, :], OP.mult),
                        waits=[("pe", pe_done)], inc="dv")
                    fm_pb_free[tgi] = pr.n("dv")
                elif kind == "C":
                    pr.op("vector", lambda e, tgi=tgi, tg=tg, t0=t0: e.tensor_tensor(
                        self.C_fm[:, t0 - WARM:t0 - WARM + 512], pb[tgi][:],
                        rs_bc[:, tg, :], OP.mult),
                        waits=[("pe", pe_done)], inc="dv")
                    fm_pb_free[tgi] = pr.n("dv")
                else:
                    pr.op("vector", lambda e, tgi=tgi, tg=tg, t0=t0: e.tensor_tensor(
                        self.dt_fm[:, t0:t0 + 512], pb[tgi][0:NH, :],
                        rs_bc[0:NH, tg, :], OP.mult),
                        waits=[("pe", pe_done)], inc="dv")
                    fm_pb_free[tgi] = pr.n("dv")

        self.dump("xnT_after", self.xnT[:])
        self.dump("B_fm", self.B_fm[:])
        self.dump("C_fm", self.C_fm[:])
        self.dump("dt_fm", self.dt_fm[:])

    def phase_p3a(self):
        pr, nc = self.pr, self.nc
        pb = self.pb
        p1_done = [("pe", pr.n("pe")), ("po", pr.n("po")), ("dv", pr.n("dv")),
                   ("ac", pr.n("ac"))]

        w3f = [self.man(f"w3f{i}", [P, KD, P], F32, 80 * 1024 + i * 8 * 1024) for i in range(2)]
        w3b = [self.man(f"w3b{i}", [P, KD, P], BF16, 96 * 1024 + i * 4 * 1024) for i in range(2)]
        sg = self.man("sg_a", [P, 4, 512], BF16, 104 * 1024)
        uw = self.man("uw_a", [P, 512], BF16, 108 * 1024)
        ast = [self.man(f"ast{i}", [P, 4, 512], BF16, 110 * 1024 + i * 4 * 1024) for i in range(2)]
        wbc = self.man("wbc", [P, D], F32, 118 * 1024)
        wmrow = self.man("wmrow", [1, D], F32, 126 * 1024)
        w3r = self.w3_h.rearrange("(k p) c -> p k c", p=P)

        ld_free = [p1_done, p1_done]
        cast_free = [0, 0]
        gbank_free = [max(0, 0)] * 4
        gbank_free_ac = [0] * 4
        ubank_free = [0] * 4
        ast_free = [0, 0]
        wbc_free = 0
        sg_free = 0

        gbank_free_dv0 = pr.n("dv")
        self.sg_free_dv = 0

        jobs = []
        for n in range(NE):
            jobs.append(("wb", n, 0))
            for f in range(11):
                jobs.append(("g", n, f))
                jobs.append(("u", n, f))
        for f in range(22):
            jobs.append(("g", NE, f))
            jobs.append(("u", NE, f))

        ji = 0
        for kind, n, f in jobs:
            if kind == "wb":
                pr.op("sync", lambda e, n=n: e.dma_start(
                    out=wmrow[:], in_=self.wmd[n, :][None, :]),
                    waits=p1_done if n == 0 else [("dv", wbc_free)], inc="dA", inck=16)
                da = pr.n("dA")
                for tg in range(4):
                    pr.op("tensor", lambda e, tg=tg: e.matmul(
                        pb[4 + tg][:], self.onesf[:], wmrow[:, tg * 512:(tg + 1) * 512],
                        start=True, stop=True),
                        waits=[("dA", da), ("dv", ubank_free[tg]),
                               ("po", sg_free)] + (p1_done if n == 0 else []),
                        inc="pe")
                    pr.op("vector", lambda e, tg=tg: e.tensor_copy(
                        wbc[:, tg * 512:(tg + 1) * 512], pb[4 + tg][:]),
                        waits=[("pe", pr.n("pe"))], inc="dv")
                    ubank_free[tg] = pr.n("dv")
                wbc_free = pr.n("dv")
                continue

            par = ji % 2
            ji += 1
            shared = (n == NE)
            if kind == "g":
                c0 = (22528 + f * P) if shared else (n * 2 * FE + f * P)
            else:
                c0 = (22528 + FS + f * P) if shared else (n * 2 * FE + FE + f * P)

            pr.op("sync", lambda e, c0=c0, par=par: e.dma_start(
                out=w3f[par][:], in_=w3r[:, :, c0:c0 + P]),
                waits=ld_free[par], inc="dW", inck=16)
            dw = pr.n("dW")
            pr.op("gpsimd", lambda e, par=par: e.tensor_copy(w3b[par][:], w3f[par][:]),
                  waits=[("dW", dw), ("pe", cast_free[par])], inc="po")
            ld_free[par] = [("po", pr.n("po"))]
            po_ready = pr.n("po")

            for tg in range(4):
                bank = tg if kind == "g" else 4 + tg

                def mm(e, par=par, tg=tg, bank=bank):
                    last = None
                    t0 = WARM + tg * 512
                    for k in range(KD):
                        last = e.matmul(pb[bank][:], w3b[par][:, k, :],
                                        self.xnT[:, k, t0:t0 + 512],
                                        start=(k == 0), stop=(k == KD - 1))
                    return last
                if kind == "g":
                    w_ = [("po", po_ready), ("ac", gbank_free_ac[tg]),
                          ("po", sg_free), ("dv", gbank_free_dv0)]
                else:
                    w_ = [("po", po_ready), ("dv", ubank_free[tg])]
                pr.op("tensor", mm, waits=w_, inc="pe")
            cast_free[par] = pr.n("pe")
            pe_done = pr.n("pe")

            if kind == "g":
                for tg in range(4):
                    pr.op("scalar", lambda e, tg=tg: e.activation(
                        sg[:, tg, :], pb[tg][:], AF.Silu),
                        waits=[("pe", pe_done), ("po", sg_free),
                               ("dv", self.sg_free_dv)], inc="ac")
                    gbank_free_ac[tg] = pr.n("ac")
            else:
                row0 = (11264 + f * P) if shared else (n * FE + f * P)
                apar = (ji // 2) % 2
                for tg in range(4):
                    if shared:
                        pr.op("vector", lambda e, tg=tg, apar=apar: e.tensor_tensor(
                            ast[apar][:, tg, :], sg[:, tg, :], pb[4 + tg][:], OP.mult),
                            waits=[("pe", pe_done), ("ac", gbank_free_ac[tg]),
                                   ("dS", ast_free[apar])], inc="dv")
                        ubank_free[tg] = pr.n("dv")
                    else:
                        pr.op("vector", lambda e, tg=tg: e.tensor_tensor(
                            uw[:], pb[4 + tg][:], wbc[:, tg * 512:(tg + 1) * 512],
                            OP.mult),
                            waits=[("pe", pe_done), ("po", pr.n("po")),
                                   ("ac", gbank_free_ac[tg])], inc="dv")
                        ubank_free[tg] = pr.n("dv")
                        pr.op("gpsimd", lambda e, tg=tg, apar=apar: e.tensor_tensor(
                            ast[apar][:, tg, :], sg[:, tg, :], uw[:], OP.mult),
                            waits=[("dv", pr.n("dv")), ("dS", ast_free[apar])], inc="po")
                if shared:
                    self.sg_free_dv = pr.n("dv")
                else:
                    sg_free = pr.n("po")
                wdep = ("dv", pr.n("dv")) if shared else ("po", pr.n("po"))
                pr.op("scalar", lambda e, row0=row0, apar=apar: e.dma_start(
                    out=self.ad[row0:row0 + P, :], in_=ast[apar][:].rearrange("p g t -> p (g t)")),
                    waits=[wdep], inc="dS", inck=16)
                ast_free[apar] = pr.n("dS")

        self.dump("ad_e0", self.ad[0:256, :])
        self.dump("ad_sh", self.ad[11264:11520, :])

    def phase_p2(self):
        pr, nc = self.pr, self.nc
        pball = [b[:] for b in self.pbank]
        p3_done = [(s, pr.n(s)) for s in list(pr.semn)]

        import os as _os
        _nal = int(_os.environ.get("P2_ALLOC", 99))
        _ac = [0]
        _orig_man = self.man
        def _man(name, shape, dt, off):
            _ac[0] += 1
            if _ac[0] > _nal:
                return _orig_man(name, shape, dt, 160 * 1024 - 16 * 1024 + 0) if False else None
            return _orig_man(name, shape, dt, off)
        self.man = _man
        h_sb = self.man("h_sb", [P, DIN], F32, 0)
        h_bf = self.man("h_bf", [P, DIN], BF16, 12 * 1024)
        y = self.man("y_sb", [P, DIN], F32, 18 * 1024)
        h0sb = self.man("h0sb", [P, DIN], F32, 18 * 1024)
        Xsil = [self.man(f"Xsil{i}", [P, JIN, P], BF16, 30 * 1024 + i * 6 * 1024)
                for i in range(2)]
        Xtok = [self.man(f"Xtok{i}", [P, DIN], BF16, 42 * 1024 + i * 6 * 1024)
                for i in range(2)]
        conv = self.man("conv", [P, JIN, P], F32, 54 * 1024)
        zld = [self.man(f"zld{i}", [P, DIN], BF16, 66 * 1024 + i * 6 * 1024)
               for i in range(2)]
        xct = [self.man(f"xct{i}", [P, JIN, 131], BF16, 78 * 1024 + i * 6656)
               for i in range(2)]
        tL8 = [self.man(f"tL8_{i}", [2, 8 * P], F32, 91 * 1024 + i * 4096)
               for i in range(2)]
        tR8 = [self.man(f"tR8_{i}", [2, 8 * P], F32, 99 * 1024 + i * 4096)
               for i in range(2)]
        lfm = self.man("lfm", [NH, TEXT], F32, 107 * 1024)
        G8 = self.man("G8", [P, 8, P], BF16, 117 * 1024)
        Wt8 = self.man("Wt8", [P, 8, P], BF16, 119 * 1024)
        sm2 = self.man("sm2", [P, 512], F32, 121 * 1024)
        G_sb = self.man("G_sb", [P, P], BF16, 123 * 1024)
        Btd8 = self.man("Btd8", [P, 8, P], BF16, 123 * 1024 + 256)
        B_tok = self.man("B_tok", [P, P], BF16, 125 * 1024 + 512)
        dca = self.man("dca", [112, P], F32, 125 * 1024 + 768)
        cscr = self.man("cscr", [P, JIN // 2, P], F32, 134 * 1024)

        self.man = _orig_man
        if any(v is None for v in [h_sb, h_bf, y, h0sb, conv, lfm, G8, Wt8, sm2,
                                   G_sb, Btd8, B_tok, dca, cscr] + Xsil + Xtok + zld
               + xct + tL8 + tR8):
            return
        scr1 = conv[:, :, :].rearrange("p j t -> p (j t)")[0:NH, 0:TEXT]
        if int(_os.environ.get("P2_AUTOS", 1)) < 1:
            return
        ssqy = self.man("ssqy", [P, 1], F32, 140 * 1024)
        yt1 = self.man("yt1", [P, 1], F32, 140 * 1024 + 32)
        ends = self.man("ends", [NH, NCH], F32, 140 * 1024 + 64)
        prev = self.man("prev", [NH, NCH], F32, 140 * 1024 + 160)
        dxrow = self.man("dxrow", [1, NH], F32, 140 * 1024 + 256)
        dt = self.dt_fm

        import os
        PRE = int(os.environ.get("P2_PRE", 9))
        if PRE < 1:
            return
        pr.op("sync", lambda e: e.dma_start(out=h0sb[:], in_=self.h0_h),
              waits=p3_done, inc="dA", inck=16)
        pr.op("gpsimd", lambda e: e.memset(h_sb[:], 0.0), waits=p3_done, inc="po")
        pr.op("gpsimd", lambda e: e.memset(h_bf[:], 0.0), inc="po")
        if PRE < 2:
            return
        pr.op("vector", lambda e: e.tensor_scalar_add(dt[:], dt[:], self.abd[:, 1:2]),
              waits=p3_done, inc="dv")
        pr.op("scalar", lambda e: e.activation(scr1, dt[:], AF.Abs),
              waits=[("dv", pr.n("dv"))] + p3_done, inc="ac")
        pr.op("scalar", lambda e: e.activation(scr1, scr1, AF.Exp, scale=-1.0),
              waits=[("ac", pr.n("ac"))], inc="ac")
        pr.op("scalar", lambda e: e.activation(scr1, scr1, AF.Ln, bias=self.cc[0:NH, 1:2]),
              waits=[("ac", pr.n("ac"))], inc="ac")
        pr.op("vector", lambda e: e.tensor_scalar_max(dt[:], dt[:], 0.0),
              waits=[("ac", pr.n("ac") - 2)], inc="dv")
        pr.op("vector", lambda e: e.tensor_tensor(dt[:], dt[:], scr1, OP.add),
              waits=[("ac", pr.n("ac")), ("dv", pr.n("dv"))], inc="dv")
        if PRE < 3:
            return
        pr.op("vector", lambda e: e.tensor_scalar_mul(scr1, dt[:], self.negA[:]),
              waits=[("dv", pr.n("dv"))], inc="dv")
        pr.op("vector", lambda e: e.tensor_tensor_scan(
            lfm[:], scr1, self.cc[0:NH, 2:3].broadcast_to([NH, TEXT]), 0.0,
            OP.add, OP.add),
            waits=[("dv", pr.n("dv"))], inc="dv")
        lfm_dv = pr.n("dv")
        if PRE < 4:
            return
        pr.op("vector", lambda e: e.tensor_copy(ends[:], lfm[:, P - 1::P]),
              waits=[("dv", lfm_dv)], inc="dv")
        pr.op("gpsimd", lambda e: e.memset(prev[:, 0:1], 0.0), inc="po")
        pr.op("vector", lambda e: e.tensor_copy(prev[:, 1:NCH], ends[:, 0:NCH - 1]),
              waits=[("dv", pr.n("dv"))], inc="dv")
        pr.op("vector", lambda e: e.tensor_tensor(
            dca[0:NH, 0:NCH], ends[:], prev[:], OP.subtract),
            waits=[("dv", pr.n("dv")), ("po", pr.n("po"))] + p3_done, inc="dv")
        pr.op("scalar", lambda e: e.activation(dca[0:NH, 0:NCH], dca[0:NH, 0:NCH], AF.Exp),
              waits=[("dv", pr.n("dv"))], inc="ac")
        if PRE < 5:
            return
        pr.op("tensor", lambda e: e.matmul(
            pball[5][0:NCH, 0:NH], dca[0:NH, 0:NCH], self.identf[0:NH, 0:NH],
            is_transpose=True, start=True, stop=True),
            waits=[("ac", pr.n("ac"))] + p3_done, inc="pe")
        pr.op("vector", lambda e: e.tensor_copy(sm2[0:NCH, 176:176 + NH],
                                                pball[5][0:NCH, 0:NH]),
              waits=[("pe", pr.n("pe"))], inc="dv")
        pr.op("scalar", lambda e: e.dma_start(out=self.dxd, in_=sm2[0:NCH, 176:176 + NH]),
              waits=[("dv", pr.n("dv"))], inc="dS", inck=16)
        dxd_ds = pr.n("dS")
        if PRE < 6:
            return
        pr.op("scalar", lambda e: e.dma_start(out=self.ld, in_=lfm[:]),
              waits=[("dv", lfm_dv)], inc="dS", inck=16)
        ld_ds = pr.n("dS")
        for i in range(2):
            pr.op("gpsimd", lambda e, i=i: e.memset(tL8[i][:], 1.0),
                  waits=p3_done, inc="po")
            pr.op("gpsimd", lambda e, i=i: e.memset(tR8[i][:], -1.0), inc="po")
        prelude_po = pr.n("po")
        ps5_state = {"pe": pr.n("pe"), "dv": pr.n("dv"), "ac": pr.n("ac")}

        self.dump("lfm", lfm[:])
        self.dump("dtsp", dt[:])

        bE = [pball[0], pball[1]]
        bYl, bYI, bS = pball[2], pball[3], pball[4]
        b5 = pball[5]
        b5b = b5.bitcast(BF16)
        bX = [pball[6].bitcast(BF16), pball[7].bitcast(BF16)]

        st = {
            "xct_rd": [p3_done, p3_done],
            "zld_rd": [[], []],
            "Xsil_rd": [p3_done, p3_done],
            "Xtok_rd": [[], []],
            "conv_rd": [("ac", 0)],
            "tileL_pe": [0, 0],
            "dxrow_pe": 0,
            "ebank_rd": [0, 0],
            "y_ac": 0,
            "ybank_dv": 0,
            "psX_ac": [0, 0],
            "hbf_pe": 0,
            "G_po": 0,
        }

        import os
        nch_lim = int(os.environ.get('P2_CHUNKS', NCH))
        MLVL = int(os.environ.get('P2_MLVL', 9))
        for c in range(nch_lim):
            par = c % 2
            main = c >= NCW
            cm = c - NCW
            zpar = cm % 2 if main else 0
            t0 = c * P

            if c == 0:
                pr.op("sync", lambda e: e.dma_start(
                    out=xct[0][:, :, 3:131],
                    in_=self.xcd.rearrange("j p t -> p j t")[:, :, 0:P]),
                    waits=p3_done, inc="dA", inck=16)
                pr.op("gpsimd", lambda e: e.memset(xct[0][:, :, 0:3], 0.0),
                      waits=p3_done, inc="po")
            else:
                pr.op("sync", lambda e, par=par, t0=t0: e.dma_start(
                    out=xct[par][:],
                    in_=self.xcd.rearrange("j p t -> p j t")[:, :, t0 - 3:t0 + P]),
                    waits=st["xct_rd"][par], inc="dA", inck=16)
            xct_da = pr.n("dA")
            if main:
                pr.op("sync", lambda e, zpar=zpar, cm=cm: e.dma_start(
                    out=zld[zpar][:], in_=self.zd[cm * P:(cm + 1) * P, :]),
                    waits=st["zld_rd"][zpar] + (p3_done if cm < 2 else []),
                    inc="dA", inck=16)
                zld_da = pr.n("dA")
            tl_da = pr.n("dA")
            pr.op("sync", lambda e, c=c: e.dma_start(
                out=dxrow[:], in_=self.dxd[c, :][None, :]),
                waits=[("dS", dxd_ds), ("pe", st["dxrow_pe"])], inc="dA", inck=16)
            dx_da = pr.n("dA")

            blend_dv = 0
            if c == NCW:
                pr.op("vector", lambda e, par=par: e.tensor_scalar_mul(
                    xct[par][:, :, 0:3], xct[par][:, :, 0:3], self.mv[:, 0:1]),
                    waits=[("dA", xct_da)], inc="dv")
                pr.op("vector", lambda e, par=par: e.scalar_tensor_tensor(
                    xct[par][:, :, 0:3], self.csfm[:], self.mv[:, 1:2],
                    xct[par][:, :, 0:3], OP.mult, OP.add),
                    waits=[("dv", pr.n("dv"))], inc="dv")
                blend_dv = pr.n("dv")

            cw, cb = self.convw, self.convb
            JH = JIN // 2
            for half in range(2):
                jl = slice(half * JH, (half + 1) * JH)
                waits0 = [("dA", xct_da)] + st["conv_rd"] + \
                    ([("dv", blend_dv)] if blend_dv else []) + \
                    (p3_done if c == 0 else [])
                pr.op("vector", lambda e, par=par, jl=jl: e.tensor_tensor(
                    conv[:, jl, :], xct[par][:, jl, 0:P],
                    cw[:, jl, 0:1].broadcast_to([P, JH, P]), OP.mult),
                    waits=waits0, inc="dv")
                pr.op("gpsimd", lambda e, par=par, jl=jl: e.tensor_tensor(
                    cscr[:, 0:JH, :], xct[par][:, jl, 1:1 + P],
                    cw[:, jl, 1:2].broadcast_to([P, JH, P]), OP.mult),
                    waits=waits0 + [("dv", pr.n("dv") - 1)], inc="po")
                pr.op("vector", lambda e, jl=jl: e.tensor_tensor(
                    conv[:, jl, :], conv[:, jl, :], cscr[:, 0:JH, :], OP.add),
                    waits=[("po", pr.n("po")), ("dv", pr.n("dv"))], inc="dv")
                pr.op("gpsimd", lambda e, par=par, jl=jl: e.tensor_tensor(
                    cscr[:, 0:JH, :], xct[par][:, jl, 2:2 + P],
                    cw[:, jl, 2:3].broadcast_to([P, JH, P]), OP.mult),
                    waits=[("dv", pr.n("dv"))], inc="po")
                pr.op("vector", lambda e, jl=jl: e.tensor_tensor(
                    conv[:, jl, :], conv[:, jl, :], cscr[:, 0:JH, :], OP.add),
                    waits=[("po", pr.n("po")), ("dv", pr.n("dv"))], inc="dv")
                pr.op("gpsimd", lambda e, par=par, jl=jl: e.tensor_tensor(
                    cscr[:, 0:JH, :], xct[par][:, jl, 3:3 + P],
                    cw[:, jl, 3:4].broadcast_to([P, JH, P]), OP.mult),
                    waits=[("dv", pr.n("dv"))], inc="po")
                pr.op("vector", lambda e, jl=jl: e.tensor_tensor(
                    conv[:, jl, :], conv[:, jl, :], cscr[:, 0:JH, :], OP.add),
                    waits=[("po", pr.n("po")), ("dv", pr.n("dv"))], inc="dv")
                pr.op("vector", lambda e, jl=jl: e.tensor_tensor(
                    conv[:, jl, :], conv[:, jl, :],
                    cb[:, jl, None].broadcast_to([P, JH, P]), OP.add),
                    waits=[("dv", pr.n("dv"))], inc="dv")
            conv_dv = pr.n("dv")
            st["xct_rd"][par] = [("dv", conv_dv), ("po", pr.n("po"))]

            pr.op("scalar", lambda e, par=par: e.activation(
                Xsil[par][:], conv[:], AF.Silu),
                waits=[("dv", conv_dv)] + st["Xsil_rd"][par], inc="ac")
            st["conv_rd"] = [("ac", pr.n("ac"))]
            xsil_ac = pr.n("ac")
            if main:
                pr.op("scalar", lambda e, zpar=zpar: e.activation(
                    zld[zpar][:], zld[zpar][:], AF.Silu),
                    waits=[("dA", zld_da)], inc="ac")
                siluz_ac = pr.n("ac")

            for w in range(3):
                bx = w % 2

                def xtp(e, par=par, w=w, bx=bx):
                    last = None
                    for i in range(8):
                        j = w * 8 + i
                        last = e.matmul(bX[bx][:, i * P:(i + 1) * P],
                                        Xsil[par][:, j, :], self.identb[:],
                                        is_transpose=True, start=(i == 0), stop=(i == 7))
                    return last
                pr.op("tensor", xtp,
                      waits=[("ac", max(xsil_ac, st["psX_ac"][bx]))], inc="pe")
                pe_h = pr.n("pe")

                def xev(e, par=par, w=w, bx=bx):
                    last = None
                    for i in range(8):
                        j = w * 8 + i
                        last = e.copy(Xtok[par][:, j * P:(j + 1) * P],
                                      bX[bx][:, i * P:(i + 1) * P])
                    return last
                pr.op("scalar", xev,
                      waits=[("pe", pe_h)] + st["Xtok_rd"][par], inc="ac")
                st["psX_ac"][bx] = pr.n("ac")
            xtok_ac = pr.n("ac")
            st["Xsil_rd"][par] = [("pe", pr.n("pe"))]

            pr.op("vector", lambda e, c=c, t0=t0: e.tensor_scalar(
                dca[0:NH, :], lfm[:, t0:t0 + P], prev[:, c:c + 1], None, OP.subtract),
                waits=[("dv", 0), ("pe", ps5_state["pe"])], inc="dv")
            pr.op("vector", lambda e, c=c, t0=t0: e.tensor_scalar(
                dca[64:64 + NH, :], lfm[:, t0:t0 + P], ends[:, c:c + 1], -1.0,
                OP.subtract, OP.mult),
                waits=[("dv", pr.n("dv") - 1)], inc="dv")
            dca_dv = pr.n("dv")
            pr.op("scalar", lambda e: e.activation(dca[0:NH, :], dca[0:NH, :], AF.Exp),
                  waits=[("dv", dca_dv)], inc="ac")
            pr.op("scalar", lambda e: e.activation(dca[64:64 + NH, :], dca[64:64 + NH, :],
                                                   AF.Exp),
                  waits=[("ac", pr.n("ac"))], inc="ac")
            dca_ac = pr.n("ac")

            def b5ops(e, c=c, main=main, cm=cm, t0=t0):
                last = e.matmul(b5[:, 0:NH], self.onesf[:], dxrow[:],
                                start=True, stop=False)
                last = e.matmul(b5[:, 64:176], dca[:], self.identf[0:112, 0:112],
                                is_transpose=True, start=False, stop=False)
                last = e.matmul(b5b[:, 640:768], self.B_fm[:, t0:t0 + P],
                                self.identb[:], is_transpose=True,
                                start=False, stop=True)
                return last
            pr.op("tensor", b5ops,
                  waits=[("dA", dx_da), ("ac", dca_ac),
                         ("dv", ps5_state["dv"]), ("ac", ps5_state["ac"])], inc="pe")
            st["dxrow_pe"] = pr.n("pe")
            b5_pe = pr.n("pe")
            pr.op("vector", lambda e: e.tensor_copy(sm2[:, 0:112], b5[:, 64:176]),
                  waits=[("pe", b5_pe)], inc="dv")
            pr.op("vector", lambda e: e.tensor_copy(sm2[:, 128:176], b5[:, 0:NH]),
                  waits=[("dv", pr.n("dv") - 1)], inc="dv")
            pr.op("vector", lambda e: e.tensor_copy(B_tok[:], b5b[:, 640:768]),
                  waits=[("dv", pr.n("dv") - 1)], inc="dv")
            if main and MLVL >= 3:
                pr.op("tensor", lambda e, t0=t0, cm=cm: e.matmul(
                    bE[0][:, 0:P], self.B_fm[:, t0:t0 + P],
                    self.C_fm[:, cm * P:(cm + 1) * P], start=True, stop=True),
                    waits=[("ac", max(st["ebank_rd"]))], inc="pe")
                pr.op("scalar", lambda e: e.copy(G_sb[:], bE[0][:, 0:P]),
                      waits=[("pe", pr.n("pe")), ("po", st["G_po"])], inc="ac")
                gsb_ac = pr.n("ac")
                st["ebank_rd"] = [pr.n("ac"), st["ebank_rd"][1]]
            b5ev_dv = pr.n("dv")
            ps5_state = {"pe": b5_pe, "dv": b5ev_dv, "ac": pr.n("ac")}

            for w in range(6):
                hh0 = w * 8
                if main and MLVL >= 2:
                    wpar = w % 2
                    lsrc = self.ld[:, t0:t0 + P][hh0:hh0 + 8][None, :, :]
                    dl = tL8[wpar][0:1, :].rearrange("o (h s) -> o h s", s=P)
                    drr = tR8[wpar][1:2, :].rearrange("o (h s) -> o h s", s=P)
                    pr.op("sync", lambda e, lsrc=lsrc, dl=dl: e.dma_start(
                        out=dl, in_=lsrc),
                        waits=[("dS", ld_ds), ("pe", st["tileL_pe"][wpar]),
                               ("po", prelude_po)], inc="dA", inck=16)
                    pr.op("sync", lambda e, lsrc=lsrc, drr=drr: e.dma_start(
                        out=drr, in_=lsrc), inc="dA", inck=16)
                    tl8_da = pr.n("dA")

                    def emms(e, hh0=hh0, wpar=wpar):
                        last = None
                        for i in range(8):
                            bank = bE[i // 4]
                            sl = i % 4
                            last = e.matmul(bank[:, sl * P:(sl + 1) * P],
                                            tL8[wpar][0:2, i * P:(i + 1) * P],
                                            tR8[wpar][0:2, i * P:(i + 1) * P],
                                            start=(sl == 0), stop=(sl == 3))
                        return last
                    pr.op("tensor", emms,
                          waits=[("dA", tl8_da), ("ac", max(st["ebank_rd"]))], inc="pe")
                    st["tileL_pe"][wpar] = pr.n("pe")
                    e_pe = pr.n("pe")

                    def gexp(e, hh0=hh0):
                        last = None
                        for i in range(8):
                            bank = bE[i // 4]
                            sl = i % 4
                            last = e.activation(G8[:, i, :],
                                                bank[:, sl * P:(sl + 1) * P], AF.Exp)
                        return last
                    pr.op("scalar", gexp, waits=[("pe", e_pe)], inc="ac")
                    st["ebank_rd"] = [pr.n("ac"), pr.n("ac")]

                    def gmask(e):
                        last = None
                        for i in range(8):
                            last = e.affine_select(
                                out=G8[:, i, :], in_=G8[:, i, :],
                                compare_op=OP.is_ge, fill=0.0, base=0,
                                pattern=[[1, P]], channel_multiplier=-1)
                        return last
                    pr.op("gpsimd", gmask, waits=[("ac", pr.n("ac"))], inc="po")

                    if MLVL >= 4:
                        def wmul(e):
                            last = None
                            for i in range(8):
                                last = e.tensor_tensor(Wt8[:, i, :], G8[:, i, :],
                                                       G_sb[:], OP.mult)
                            return last
                        pr.op("gpsimd", wmul,
                              waits=[("po", pr.n("po")), ("ac", gsb_ac)], inc="po")
                        wt_po = pr.n("po")
                        st["G_po"] = wt_po

                def btd(e, hh0=hh0):
                    last = None
                    for i in range(8):
                        h = hh0 + i
                        last = e.tensor_scalar_mul(Btd8[:, i, :], B_tok[:],
                                                   sm2[:, 64 + h:65 + h])
                    return last
                pr.op("vector", btd, waits=[("dv", b5ev_dv), ("pe", pr.n("pe"))],
                      inc="dv")
                btd_dv = pr.n("dv")

                def wavemm(e, hh0=hh0, main=main, cm=cm, par=par):
                    last = None
                    if main and MLVL >= 5:
                        for i in range(8):
                            h = hh0 + i
                            last = e.matmul(bYl[:, i * DH:(i + 1) * DH],
                                            Wt8[:, i, :],
                                            Xtok[par][:, h * DH:(h + 1) * DH],
                                            start=(i == 0), stop=(i == 7))
                        for i in range(8):
                            h = hh0 + i
                            last = e.matmul(bYI[:, i * DH:(i + 1) * DH],
                                            self.C_fm[:, cm * P:(cm + 1) * P],
                                            h_bf[:, h * DH:(h + 1) * DH],
                                            start=(i == 0), stop=(i == 7))
                    for i in range(8):
                        h = hh0 + i
                        last = e.matmul(bS[:, i * DH:(i + 1) * DH],
                                        Btd8[:, i, :],
                                        Xtok[par][:, h * DH:(h + 1) * DH],
                                        start=(i == 0), stop=(i == 7))
                    return last
                mmw = [("dv", max(btd_dv, st["ybank_dv"])), ("ac", xtok_ac)]
                if main and MLVL >= 5:
                    mmw.append(("po", wt_po))
                pr.op("tensor", wavemm, waits=mmw, inc="pe")
                wave_pe = pr.n("pe")
                st["hbf_pe"] = wave_pe

                def stts(e, hh0=hh0, main=main, par=par):
                    last = None
                    for i in range(8):
                        h = hh0 + i
                        if main and MLVL >= 6:
                            last = e.scalar_tensor_tensor(
                                y[:, h * DH:(h + 1) * DH],
                                Xtok[par][:, h * DH:(h + 1) * DH],
                                self.Dbc[:, h:h + 1],
                                bYl[:, i * DH:(i + 1) * DH], OP.mult, OP.add)
                            last = e.scalar_tensor_tensor(
                                y[:, h * DH:(h + 1) * DH],
                                bYI[:, i * DH:(i + 1) * DH],
                                sm2[:, h:h + 1],
                                y[:, h * DH:(h + 1) * DH], OP.mult, OP.add)
                        last = e.scalar_tensor_tensor(
                            h_sb[:, h * DH:(h + 1) * DH],
                            h_sb[:, h * DH:(h + 1) * DH],
                            sm2[:, 128 + h:129 + h],
                            bS[:, i * DH:(i + 1) * DH], OP.mult, OP.add)
                    return last
                pr.op("vector", stts,
                      waits=[("pe", wave_pe), ("ac", st["y_ac"] if main and w == 0 else 0),
                             ("po", 0)], inc="dv")
                st["ybank_dv"] = pr.n("dv")

                def hbfc(e, hh0=hh0):
                    last = None
                    for i in range(8):
                        h = hh0 + i
                        last = e.tensor_copy(h_bf[:, h * DH:(h + 1) * DH],
                                             h_sb[:, h * DH:(h + 1) * DH])
                    return last
                pr.op("gpsimd", hbfc,
                      waits=[("dv", pr.n("dv")), ("pe", wave_pe)], inc="po")
            hbf_po = pr.n("po")
            sts_dv = pr.n("dv")

            if not main:
                st["Xtok_rd"][par] = [("pe", pr.n("pe")), ("dv", sts_dv)]

            if c == NCW:
                self.dump("c4_Gsb", G_sb[:])
                self.dump("c4_Wt8", Wt8[:])
                self.dump("c4_G8", G8[:])
                self.dump("c4_sm2", sm2[:])
                self.dump("c4_y", y[:])
                self.dump("c4_X", Xtok[par][:])
                self.dump("c4_hbf", h_bf[:])

            if c == NCW - 1:
                pr.op("vector", lambda e: e.tensor_scalar_mul(
                    h_sb[:], h_sb[:], self.mv[:, 0:1]),
                    waits=[("dv", sts_dv), ("dA", pr.n("dA"))], inc="dv")
                pr.op("vector", lambda e: e.scalar_tensor_tensor(
                    h_sb[:], h0sb[:], self.mv[:, 1:2], h_sb[:], OP.mult, OP.add),
                    waits=[("dv", pr.n("dv"))], inc="dv")
                pr.op("gpsimd", lambda e: e.tensor_copy(h_bf[:], h_sb[:]),
                      waits=[("dv", pr.n("dv")), ("pe", pr.n("pe"))], inc="po")

            if main and MLVL >= 7:
                pr.op("scalar", lambda e, par=par: e.activation(
                    Xtok[par][:], y[:], AF.Square, accum_out=ssqy[:]),
                    waits=[("dv", sts_dv)], inc="ac")
                st["y_ac"] = pr.n("ac")
                st["Xtok_rd"][par] = [("ac", pr.n("ac")), ("dv", sts_dv),
                                      ("pe", pr.n("pe"))]
                pr.op("scalar", lambda e: e.activation(
                    yt1[:], ssqy[:], AF.Ln, bias=self.cc[:, 0:1], scale=1.0 / DIN),
                    waits=[("ac", pr.n("ac"))], inc="ac")
                pr.op("scalar", lambda e: e.activation(
                    yt1[:], yt1[:], AF.Exp, scale=-0.5),
                    waits=[("ac", pr.n("ac"))], inc="ac")
                rstd_ac = pr.n("ac")
                pr.op("vector", lambda e, zpar=zpar: e.scalar_tensor_tensor(
                    zld[zpar][:], y[:], yt1[:], zld[zpar][:], OP.mult, OP.mult),
                    waits=[("ac", max(rstd_ac, siluz_ac))], inc="dv")
                yf_dv = pr.n("dv")

                for w in range(3):
                    bx = w % 2

                    def ytp(e, zpar=zpar, w=w, bx=bx):
                        last = None
                        for i in range(8):
                            j = w * 8 + i
                            last = e.matmul(bX[bx][:, i * P:(i + 1) * P],
                                            zld[zpar][:, j * P:(j + 1) * P],
                                            self.identb[:], is_transpose=True,
                                            start=(i == 0), stop=(i == 7))
                        return last
                    pr.op("tensor", ytp,
                          waits=[("dv", yf_dv), ("ac", st["psX_ac"][bx])], inc="pe")
                    pe_h = pr.n("pe")

                    def yev(e, par=par, w=w, bx=bx):
                        last = None
                        for i in range(8):
                            j = w * 8 + i
                            last = e.copy(Xsil[par][:, j, :],
                                          bX[bx][:, i * P:(i + 1) * P])
                        return last
                    pr.op("scalar", yev, waits=[("pe", pe_h)], inc="ac")
                    st["psX_ac"][bx] = pr.n("ac")
                st["zld_rd"][zpar] = [("pe", pr.n("pe"))]
                ytp_pe = pr.n("pe")
                pr.op("scalar", lambda e, par=par, cm=cm: e.dma_start(
                    out=self.ygd.rearrange("j p t -> p j t")[:, :, cm * P:(cm + 1) * P],
                    in_=Xsil[par][:]),
                    waits=[("ac", pr.n("ac"))], inc="dS", inck=16)
                st["Xsil_rd"][par] = [("pe", ytp_pe), ("dS", pr.n("dS"))]

        self.dump("h_final", h_sb[:])
        self.dump("y_last", y[:])
        self.dump("ygd", self.ygd)

    def phase_p4(self):
        pr, nc = self.pr, self.nc
        pb = [b[:] for b in self.pbank]
        p2_done = [(s, pr.n(s)) for s in list(pr.semn)]

        wdf = [self.man(f"wdf{i}", [P, 512], F32, i * 2048) for i in range(2)]
        wdb = [self.man(f"wdb{i}", [P, 512], BF16, 4096 + i * 1024) for i in range(2)]
        atile = [self.man(f"atile{i}", [P, 1024], BF16, 6144 + i * 2048) for i in range(2)]
        g_sb = self.man("g_sb4", [P, 8, 512], BF16, 10 * 1024)
        xld4 = [self.man(f"xld4_{i}", [P, 512], F32, 18 * 1024 + i * 2048) for i in range(2)]
        xnb = self.man("xnb4", [P, 512], F32, 22 * 1024)
        outb = [self.man(f"outb{i}", [P, 512], F32, 24 * 1024 + i * 2048) for i in range(2)]
        ygt = [self.man(f"ygt{i}", [P, 1024], BF16, 28 * 1024 + i * 2048) for i in range(2)]

        wld = [p2_done, p2_done]
        wcast = [0, 0]
        ald = [p2_done, p2_done]
        bank_dv = p2_done
        g_po = 0
        out_ds = [0, 0]
        xldw = [p2_done, p2_done]

        NF = FDOWN // P
        NO = DIN // P

        for tg in range(2):
            for do in range(4):
                for f in range(NF):
                    par = f % 2
                    pr.op("sync", lambda e, f=f, do=do, par=par: e.dma_start(
                        out=wdf[par][:],
                        in_=self.wd_h[f * P:(f + 1) * P, do * 512:(do + 1) * 512]),
                        waits=wld[par], inc="dW", inck=16)
                    dw = pr.n("dW")
                    pr.op("gpsimd", lambda e, par=par: e.tensor_copy(
                        wdb[par][:], wdf[par][:]),
                        waits=[("dW", dw), ("pe", wcast[par])], inc="po")
                    wld[par] = [("po", pr.n("po"))]
                    po_r = pr.n("po")
                    pr.op("sync", lambda e, f=f, tg=tg, par=par: e.dma_start(
                        out=atile[par][:],
                        in_=self.ad[f * P:(f + 1) * P, tg * 1024:(tg + 1) * 1024]),
                        waits=ald[par], inc="dA", inck=16)
                    da = pr.n("dA")

                    def dmm(e, f=f, par=par):
                        last = None
                        for tc in range(8):
                            last = e.matmul(pb[tc][:],
                                            atile[par][:, tc * P:(tc + 1) * P],
                                            wdb[par][:],
                                            start=(f == 0), stop=(f == NF - 1))
                        return last
                    pr.op("tensor", dmm,
                          waits=[("po", po_r), ("dA", da)] +
                          (bank_dv if f == 0 else []), inc="pe")
                    wcast[par] = pr.n("pe")
                    ald[par] = [("pe", pr.n("pe"))]
                down_pe = pr.n("pe")

                for tc in range(8):
                    pr.op("scalar", lambda e, tc=tc: e.activation(
                        g_sb[:, tc, :], pb[tc][:], AF.Silu),
                        waits=[("pe", down_pe), ("dv", g_po)], inc="ac")
                silu_ac = pr.n("ac")

                for fo in range(NO):
                    par = fo % 2
                    pr.op("sync", lambda e, fo=fo, do=do, par=par: e.dma_start(
                        out=wdf[par][:],
                        in_=self.wo_h[fo * P:(fo + 1) * P, do * 512:(do + 1) * 512]),
                        waits=wld[par], inc="dW", inck=16)
                    dw = pr.n("dW")
                    pr.op("gpsimd", lambda e, par=par, fo=fo: e.tensor_scalar_mul(
                        wdb[par][:], wdf[par][:], self.innerw[:, fo:fo + 1]),
                        waits=[("dW", dw), ("pe", wcast[par])], inc="po")
                    wld[par] = [("po", pr.n("po"))]
                    po_r = pr.n("po")
                    pr.op("sync", lambda e, fo=fo, tg=tg, par=par: e.dma_start(
                        out=ygt[par][:],
                        in_=self.ygd[fo, :, tg * 1024:(tg + 1) * 1024]),
                        waits=ald[par], inc="dA", inck=16)
                    da = pr.n("dA")

                    def omm(e, fo=fo, par=par):
                        last = None
                        for tc in range(8):
                            last = e.matmul(pb[tc][:],
                                            ygt[par][:, tc * P:(tc + 1) * P],
                                            wdb[par][:],
                                            start=(fo == 0), stop=(fo == NO - 1))
                        return last
                    pr.op("tensor", omm,
                          waits=[("po", po_r), ("dA", da)] +
                          ([("ac", silu_ac)] if fo == 0 else []), inc="pe")
                    wcast[par] = pr.n("pe")
                    ald[par] = [("pe", pr.n("pe"))]
                oproj_pe = pr.n("pe")

                dv_marks = []
                for tc in range(8):
                    xpar = tc % 2
                    c = NCW + tg * 8 + tc
                    r0 = WARM + tg * 1024 + tc * P
                    pr.op("sync", lambda e, r0=r0, do=do, xpar=xpar: e.dma_start(
                        out=xld4[xpar][:],
                        in_=self.x_h[r0:r0 + P, do * 512:(do + 1) * 512]),
                        waits=xldw[xpar], inc="dX", inck=16)
                    dxh = pr.n("dX")

                    def comb(e, tc=tc, c=c, xpar=xpar, do=do):
                        e.tensor_scalar(xnb[:], xld4[xpar][:],
                                        self.mrstd[:, 2 * c:2 * c + 1],
                                        self.mrstd[:, 2 * c + 1:2 * c + 2],
                                        OP.subtract, OP.mult)
                        e.tensor_tensor(xnb[:], xnb[:], pb[tc][:], OP.add)
                        e.tensor_tensor(xnb[:], xnb[:], g_sb[:, tc, :], OP.mult)
                        return e.tensor_tensor(outb[tc % 2][:], xnb[:],
                                               xld4[xpar][:], OP.add)
                    pr.op("vector", comb,
                          waits=[("dX", dxh), ("pe", oproj_pe),
                                 ("dS", out_ds[tc % 2])], inc="dv")
                    xldw[xpar] = [("dv", pr.n("dv"))]
                    dv_marks.append(pr.n("dv"))
                    pr.op("scalar", lambda e, r0=r0, do=do, tc=tc: e.dma_start(
                        out=self.out_h[r0 - WARM:r0 - WARM + P,
                                       do * 512:(do + 1) * 512],
                        in_=outb[tc % 2][:]),
                        waits=[("dv", pr.n("dv"))], inc="dS", inck=16)
                    out_ds[tc % 2] = pr.n("dS")
                bank_dv = [("dv", pr.n("dv"))]
                g_po = pr.n("dv")


def S_g(gsc, g):
    return gsc[:, :, g]


def build_program(dbg=None, phases=("C", "P0", "P1", "P3a", "P2", "P4")):
    k = Kernel(dbg)
    if "C" in phases:
        k.phase_consts()
    if "P0" in phases:
        k.phase_p0()
    if "P1" in phases:
        k.phase_p1()
        k.dump("zd", k.zd)
        k.dump("xcd", k.xcd)
    if "P3a" in phases:
        k.phase_p3a()
    if "P2" in phases:
        k.phase_p2()
    if "P4" in phases:
        k.phase_p4()
    pr = k.pr
    waits = [(s, pr.n(s)) for s in list(pr.semn)]
    pr.op("sync", lambda e: e.nop(), waits=waits)

    with k.nc.Block() as blk:
        pr.emit(blk)
    return k


def kernel(**inputs):
    maps = prep_inputs(inputs)
    k = build_program()
    res = run_bass_kernel_spmd(k.nc, maps, core_ids=list(range(8)))
    out = np.zeros((2, 8192, 2048), np.float32)
    for core in range(8):
        b, q = core // 4, core % 4
        out[b, q * TMAIN:(q + 1) * TMAIN] = res.results[core]["out"]
    return out


def prep_inputs(inputs):
    x = np.ascontiguousarray(inputs["L0_attn_x"])
    w1t = np.ascontiguousarray(np.asarray(inputs["L0_attn_in_proj_w"]).T)
    gate_w = np.asarray(inputs["L0_ffn_gate_w"])
    up_w = np.asarray(inputs["L0_ffn_up_w"])
    parts = []
    for n in range(NE):
        parts.append(gate_w[n].T)
        parts.append(up_w[n].T)
    parts.append(np.asarray(inputs["L0_ffn_sh_gate_w"]).T)
    parts.append(np.asarray(inputs["L0_ffn_sh_up_w"]).T)
    w3t = np.ascontiguousarray(np.concatenate(parts, axis=1))
    dparts = [np.ascontiguousarray(np.asarray(inputs["L0_ffn_down_w"])[n].T) for n in range(NE)]
    dparts.append(np.ascontiguousarray(np.asarray(inputs["L0_ffn_sh_down_w"]).T))
    wdt = np.ascontiguousarray(np.concatenate(dparts, axis=0))
    wot = np.ascontiguousarray(np.asarray(inputs["L0_attn_out_proj_w"]).T)
    gwt = np.ascontiguousarray(np.asarray(inputs["L0_ffn_gate_weight"]).T)
    normw_kp = np.ascontiguousarray(np.asarray(inputs["L0_attn_norm_w"]).reshape(KD, P).T)
    innerw_kp = np.ascontiguousarray(np.asarray(inputs["L0_attn_inner_norm_w"]).reshape(JIN, P).T)
    convw_fm = np.ascontiguousarray(
        np.asarray(inputs["L0_attn_conv_w"]).reshape(JIN, P, 4).transpose(1, 0, 2))
    convb_fm = np.ascontiguousarray(np.asarray(inputs["L0_attn_conv_b"]).reshape(JIN, P).T)
    cs_fm = np.ascontiguousarray(
        np.asarray(inputs["L0_attn_conv_state"])[0].reshape(JIN, P, 3).transpose(1, 0, 2))
    h0_fm = np.ascontiguousarray(
        np.asarray(inputs["L0_attn_ssm_state"])[0].transpose(1, 0, 2).reshape(DS, NH * DH))
    abd = np.zeros((NH, 4), np.float32)
    abd[:, 0] = np.asarray(inputs["L0_attn_A_log"])
    abd[:, 1] = np.asarray(inputs["L0_attn_dt_bias"])
    abd[:, 2] = np.asarray(inputs["L0_attn_D_param"])

    shared = dict(w1t=w1t, w3t=w3t, wdt=wdt, wot=wot, gwt=gwt,
                  normw_kp=normw_kp, innerw_kp=innerw_kp, convw_fm=convw_fm,
                  convb_fm=convb_fm, cs_fm=cs_fm, h0_fm=h0_fm, abd=abd)
    maps = []
    for core in range(8):
        b, q = core // 4, core % 4
        s0 = q * TMAIN
        if q == 0:
            x_ext = np.concatenate([np.zeros((WARM, D), np.float32), x[b, :TMAIN]], axis=0)
            mval = 0.0
        else:
            x_ext = np.ascontiguousarray(x[b, s0 - WARM:s0 + TMAIN])
            mval = 1.0
        mvec = np.empty((P, 2), np.float32)
        mvec[:, 0] = mval
        mvec[:, 1] = 1.0 - mval
        m = dict(shared)
        m["x_ext"] = np.ascontiguousarray(x_ext, dtype=np.float32)
        m["mvec"] = mvec
        maps.append(m)
    return maps


# version 05 (final)
# speedup vs baseline: 1.0297x; 1.0297x over previous
import numpy as np
import ml_dtypes
import concourse.bass as bass
import concourse.mybir as mybir
from concourse.bass_utils import run_bass_kernel_spmd

F32 = mybir.dt.float32
BF16 = mybir.dt.bfloat16
AF = mybir.ActivationFunctionType
OP = mybir.AluOpType
AX = mybir.AxisListType

P = 128
D = 2048
KD = 16
WARM = 512
TMAIN = 2048
TEXT = WARM + TMAIN
NCH = TEXT // P
NCW = WARM // P
NCM = TMAIN // P
DIN = 3072
JIN = DIN // P
NH = 48
DH = 64
DS = 128
NPROJ = 6448
NE = 8
FE = 1408
FS = 2816
FDOWN = NE * FE + FS
NW3 = 2 * NE * FE + 2 * FS
EPS = 1e-5

ENGINES = ("tensor", "vector", "scalar", "gpsimd", "sync")


class Prog:

    def __init__(self, nc):
        self.nc = nc
        self.semh = {}
        self.semn = {}
        self.ops = {e: [] for e in ENGINES}

    def sem(self, name):
        if name not in self.semh:
            self.semh[name] = self.nc.alloc_semaphore(name)
            self.semn[name] = 0
        return name

    def n(self, name):
        return self.semn[self.sem(name)]

    def op(self, eng, fn, waits=(), inc=None, inck=1):
        w = tuple((self.sem(s), int(v)) for s, v in waits if v > 0)
        if inc is not None:
            self.sem(inc)
            self.semn[inc] += inck
        self.ops[eng].append((fn, w, inc, inck))
        return self.semn[inc] if inc is not None else None

    def emit(self, blk):
        for eng in ENGINES:
            oplist = self.ops[eng]
            if not oplist:
                continue

            @getattr(blk, eng)
            def _(e, oplist=oplist):
                seen = {}
                for fn, waits, inc, inck in oplist:
                    for s, v in waits:
                        if seen.get(s, 0) >= v:
                            continue
                        e.wait_ge(self.semh[s], v)
                        seen[s] = v
                    ins = fn(e)
                    if inc is not None:
                        if ins is None:
                            ins = e.nop()
                        ins.then_inc(self.semh[inc], inck)


def ceil_div(a, b):
    return (a + b - 1) // b


class Kernel:
    def __init__(self, dbg=None):
        self.dbg = set(dbg or ())
        self.dbg_names = []
        nc = bass.Bass("TRN2", target_bir_lowering=False)
        self.nc = nc
        self.pr = Prog(nc)

        dr = nc.dram_tensor
        self.x_h = dr("x_ext", [TEXT, D], F32, kind="ExternalInput").ap()
        self.w1_h = dr("w1t", [D, NPROJ], F32, kind="ExternalInput").ap()
        self.w3_h = dr("w3t", [D, NW3], F32, kind="ExternalInput").ap()
        self.wd_h = dr("wdt", [FDOWN, D], F32, kind="ExternalInput").ap()
        self.wo_h = dr("wot", [DIN, D], F32, kind="ExternalInput").ap()
        self.gw_h = dr("gwt", [D, NE], F32, kind="ExternalInput").ap()
        self.normw_h = dr("normw_kp", [P, KD], F32, kind="ExternalInput").ap()
        self.innerw_h = dr("innerw_kp", [P, JIN], F32, kind="ExternalInput").ap()
        self.convw_h = dr("convw_fm", [P, JIN, 4], F32, kind="ExternalInput").ap()
        self.convb_h = dr("convb_fm", [P, JIN], F32, kind="ExternalInput").ap()
        self.cs_h = dr("cs_fm", [P, JIN, 3], F32, kind="ExternalInput").ap()
        self.h0_h = dr("h0_fm", [DS, NH * DH], F32, kind="ExternalInput").ap()
        self.abd_h = dr("abd", [NH, 4], F32, kind="ExternalInput").ap()
        self.m_h = dr("mvec", [P, 2], F32, kind="ExternalInput").ap()

        self.out_h = dr("out", [TMAIN, D], F32, kind="ExternalOutput").ap()

        self.zd = dr("z_scr", [TMAIN, DIN], BF16).ap()
        self.xcd = dr("xc_scr", [JIN, P, TEXT], BF16).ap()
        self.ad = dr("a_scr", [FDOWN, TMAIN], BF16).ap()
        self.ygd = dr("yg_scr", [JIN, P, TMAIN], BF16).ap()
        self.ld = dr("l_scr", [NH, TEXT], F32).ap()
        self.wmd = dr("wm_scr", [NE, TMAIN], F32).ap()
        self.dxd = dr("dx_scr", [NCH, NH], F32).ap()
        self.rsd = dr("rs_scr", [NCH, P], F32).ap()

        A = {}

        def sb(name, shape, dt):
            t = nc.alloc_sbuf_tensor(name, shape, dt)
            A[name] = t
            return t

        self.identb = sb("identb", [P, P], BF16)
        self.identf = sb("identf", [P, P], F32)
        self.onesf = sb("onesf", [1, P], F32)
        self.cc = sb("constcol", [P, 4], F32)
        self.normw = sb("normw", [P, KD], F32)
        self.innerw = sb("innerw", [P, JIN], F32)
        self.convw = sb("convw", [P, JIN, 4], F32)
        self.convb = sb("convb", [P, JIN], F32)
        self.csfm = sb("csfm", [P, JIN, 3], F32)
        self.abd = sb("abd_sb", [NH, 4], F32)
        self.negA = sb("negA", [NH, 1], F32)
        self.Dbc = sb("Dbc", [P, NH], F32)
        self.mv = sb("mv", [P, 2], F32)
        self.gwf = sb("gwf", [P, KD, NE], F32)
        self.gwhi = sb("gwhi", [P, KD, NE], BF16)
        self.gwlo = sb("gwlo", [P, KD, NE], BF16)
        self.mrstd = sb("mrstd", [P, 2 * NCH], F32)
        self.rstok = sb("rstok", [P, NCH], F32)
        self.scores = sb("scores", [P, NCM, NE], F32)
        self.wmat = sb("wmat", [P, NCM, NE], F32)

        self.pbank = [nc.alloc_psum_tensor(f"bank{i}", [P, 512], F32) for i in range(8)]

        BASE = 82 * 1024
        self.man_off = BASE
        self.man_map = {}

        self.xnT = self.man("xnT", [P, KD, TEXT], BF16, 0)

        self.built = False

    def man(self, name, shape, dt, off_rel):
        nbytes = int(np.prod(shape[1:])) * mybir.dt.size(dt)
        t = self.nc.alloc_sbuf_tensor_at(name, shape, dt, offset=self.man_off + off_rel)
        self.man_map[name] = (off_rel, nbytes)
        return t

    def dump(self, name, ap, shape=None, dt=None):
        if name not in self.dbg:
            return
        nc, pr = self.nc, self.pr
        shape = shape or list(ap.shape)
        dt = dt or ap.dtype
        o = nc.dram_tensor("dbg_" + name, shape, dt, kind="ExternalOutput").ap()
        self.dbg_names.append("dbg_" + name)
        waits = [(s, pr.n(s)) for s in list(pr.semn)]
        pr.op("sync", lambda e, ap=ap, o=o: e.dma_start(out=o, in_=ap),
              waits=waits, inc="dS", inck=16)

    def barrier(self, tag):
        pr = self.pr
        waits = [(s, pr.n(s)) for s in list(pr.semn)]
        for eng in ENGINES:
            pr.op(eng, lambda e: e.nop(), waits=waits, inc="bar")

    def phase_consts(self):
        pr, nc = self.pr, self.nc

        for dst, src in [
            (self.normw[:], self.normw_h), (self.innerw[:], self.innerw_h),
            (self.convw[:], self.convw_h), (self.convb[:], self.convb_h),
            (self.csfm[:], self.cs_h), (self.abd[:], self.abd_h),
            (self.mv[:], self.m_h),
            (self.gwf[:], self.gw_h.rearrange("(k p) e -> p k e", p=P)),
        ]:
            pr.op("sync", lambda e, dst=dst, src=src: e.dma_start(out=dst, in_=src),
                  inc="dW", inck=16)

        def idents(e):
            e.memset(self.identb[:], 0.0)
            e.memset(self.identf[:], 0.0)
            e.memset(self.onesf[:], 1.0)
            e.memset(self.cc[:, 0:1], EPS)
            e.memset(self.cc[:, 1:2], 1.0)
            e.memset(self.cc[:, 2:3], 0.0)
            e.affine_select(out=self.identb[:], in_=self.identb[:],
                            compare_op=OP.not_equal, fill=1.0, base=0,
                            pattern=[[-1, P]], channel_multiplier=1)
            return e.affine_select(out=self.identf[:], in_=self.identf[:],
                                   compare_op=OP.not_equal, fill=1.0, base=0,
                                   pattern=[[-1, P]], channel_multiplier=1)

        pr.op("gpsimd", idents, inc="po")

        def gwsplit(e):
            e.tensor_copy(self.gwhi[:], self.gwf[:])
            return e.tensor_tensor(self.gwlo[:], self.gwf[:], self.gwhi[:], OP.subtract)

        pr.op("gpsimd", gwsplit, waits=[("dW", pr.n("dW"))], inc="po")

        pr.op("scalar", lambda e: e.activation(self.negA[:], self.abd[:, 0:1], AF.Exp),
              waits=[("dW", pr.n("dW"))], inc="ac")
        pr.op("vector", lambda e: e.tensor_scalar_mul(self.negA[:], self.negA[:], -1.0),
              waits=[("ac", pr.n("ac"))], inc="dv")

        self.psK = self.pbank[0]
        drow = nc.alloc_sbuf_tensor("drow", [1, NH], F32)
        pr.op("tensor", lambda e: e.matmul(
            self.psK[0:1, 0:NH], self.abd[:, 2:3], self.identf[0:NH, 0:NH],
            is_transpose=True, start=True, stop=True),
            waits=[("dW", pr.n("dW")), ("po", pr.n("po"))], inc="pe")
        pr.op("vector", lambda e: e.tensor_copy(drow[:], self.psK[0:1, 0:NH]),
              waits=[("pe", pr.n("pe"))], inc="dv")
        pr.op("tensor", lambda e: e.matmul(
            self.psK[:, 64:64 + NH], self.onesf[:, :], drow[:], start=True, stop=True),
            waits=[("dv", pr.n("dv"))], inc="pe")
        pr.op("vector", lambda e: e.tensor_copy(self.Dbc[:], self.psK[:, 64:64 + NH]),
              waits=[("pe", pr.n("pe"))], inc="dv")

    def phase_p0(self):
        pr, nc = self.pr, self.nc
        xld = [self.man("xld0", [P, D], F32, 80 * 1024),
               self.man("xld1", [P, D], F32, 88 * 1024)]
        tf = self.man("tf", [P, D], F32, 96 * 1024)
        hi = self.man("hi", [P, D], BF16, 104 * 1024)
        lo = self.man("lo", [P, D], BF16, 108 * 1024)
        loT = self.man("loT", [P, KD * P], BF16, 112 * 1024)
        sqs = self.man("sqs", [P, D], BF16, 116 * 1024)
        srow = nc.alloc_sbuf_tensor("srow", [P, 1], F32)
        ssq = nc.alloc_sbuf_tensor("ssq", [P, 1], F32)
        t1 = nc.alloc_sbuf_tensor("t1_", [P, 1], F32)
        t2 = nc.alloc_sbuf_tensor("t2_", [P, 1], F32)
        lgsb = nc.alloc_sbuf_tensor("lgsb", [NE, P], F32)
        psTa = self.pbank[1][:].bitcast(BF16)
        psTb = self.pbank[2][:].bitcast(BF16)
        psL = self.pbank[3][0:NE, 0:P]

        xfree = [0, 0]
        xfree_ac = [0, 0]
        tpfree = [0, 0]
        self.pe_hilo = 0

        import os
        nch_lim = int(os.environ.get('P2_CHUNKS', NCH))
        MLVL = int(os.environ.get('P2_MLVL', 9))
        for c in range(nch_lim):
            par = c % 2
            main = c >= NCW
            cm = c - NCW

            pr.op("sync", lambda e, c=c, par=par: e.dma_start(
                out=xld[par][:], in_=self.x_h[c * P:(c + 1) * P, :]),
                waits=[("dv", xfree[par]), ("ac", xfree_ac[par])], inc="dX", inck=16)
            dx_here = pr.n("dX")

            pr.op("vector", lambda e, par=par: e.reduce_sum(srow[:], xld[par][:], axis=AX.X),
                  waits=[("dX", dx_here)], inc="dv")
            pr.op("scalar", lambda e, par=par: e.activation(
                sqs[:], xld[par][:], AF.Square, accum_out=ssq[:]),
                waits=[("dX", dx_here)], inc="ac")
            xfree_ac[par] = pr.n("ac")

            mcol = self.mrstd[:, 2 * c:2 * c + 1]
            rcol = self.mrstd[:, 2 * c + 1:2 * c + 2]
            pr.op("vector", lambda e, mcol=mcol: e.tensor_scalar_mul(mcol, srow[:], 1.0 / D),
                  waits=[("ac", pr.n("ac")), ("dv", pr.n("dv"))], inc="dv")
            pr.op("vector", lambda e: e.tensor_scalar_mul(t1[:], ssq[:], 1.0 / D),
                  waits=[("dv", pr.n("dv"))], inc="dv")
            pr.op("vector", lambda e, mcol=mcol: e.tensor_tensor(t2[:], mcol, mcol, OP.mult),
                  waits=[("dv", pr.n("dv"))], inc="dv")
            pr.op("vector", lambda e: e.tensor_sub(t1[:], t1[:], t2[:]),
                  waits=[("dv", pr.n("dv"))], inc="dv")
            pr.op("scalar", lambda e: e.activation(t2[:], t1[:], AF.Ln, bias=self.cc[:, 0:1]),
                  waits=[("dv", pr.n("dv"))], inc="ac")
            pr.op("scalar", lambda e, rcol=rcol: e.activation(rcol, t2[:], AF.Exp, scale=-0.5),
                  waits=[("ac", pr.n("ac"))], inc="ac")
            pr.op("vector", lambda e, rcol=rcol: e.tensor_tensor(t2[:], rcol, rcol, OP.mult),
                  waits=[("ac", pr.n("ac"))], inc="dv")
            pr.op("vector", lambda e: e.tensor_mul(t2[:], t2[:], t1[:]),
                  waits=[("dv", pr.n("dv"))], inc="dv")
            pr.op("scalar", lambda e: e.activation(t2[:], t2[:], AF.Ln, bias=self.cc[:, 0:1]),
                  waits=[("dv", pr.n("dv"))], inc="ac")
            pr.op("scalar", lambda e, c=c: e.activation(
                self.rstok[:, c:c + 1], t2[:], AF.Exp, scale=-0.5),
                waits=[("ac", pr.n("ac"))], inc="ac")

            def xn_ops(e, par=par, main=main, mcol=mcol, rcol=rcol):
                e.tensor_scalar(tf[:], xld[par][:], mcol, rcol, OP.subtract, OP.mult)
                e.tensor_copy(hi[:], tf[:])
                if main:
                    return e.tensor_tensor(lo[:], tf[:], hi[:], OP.subtract)
                return e.engine_nop()
            pr.op("vector", xn_ops,
                  waits=[("ac", pr.n("ac")), ("pe", self.pe_hilo)], inc="dv")
            xfree[par] = pr.n("dv")

            waves = [(hi, 0, psTa), (hi, 8, psTb)] + ([(lo, 0, psTa), (lo, 8, psTb)] if main else [])
            for wi, (src, k0, bank) in enumerate(waves):
                bfree = tpfree[0] if bank is psTa else tpfree[1]

                def tp(e, src=src, k0=k0, bank=bank):
                    last = None
                    for kk in range(8):
                        k = k0 + kk
                        last = e.matmul(bank[:, kk * P:(kk + 1) * P],
                                        src[:, k * P:(k + 1) * P], self.identb[:],
                                        is_transpose=True, start=(kk == 0), stop=(kk == 7))
                    return last
                pr.op("tensor", tp, waits=[("dv", pr.n("dv")), ("ac", bfree)], inc="pe")
                pe_here = pr.n("pe")

                def ev(e, src=src, k0=k0, bank=bank, c=c):
                    last = None
                    for kk in range(8):
                        k = k0 + kk
                        dst = (self.xnT[:, k, c * P:(c + 1) * P] if src is hi
                               else loT[:, k * P:(k + 1) * P])
                        last = e.copy(dst, bank[:, kk * P:(kk + 1) * P])
                    return last
                pr.op("scalar", ev, waits=[("pe", pe_here)], inc="ac")
                if bank is psTa:
                    tpfree[0] = pr.n("ac")
                else:
                    tpfree[1] = pr.n("ac")

            self.pe_hilo = pr.n("pe")
            if main:
                def logit_mms(e, c=c):
                    last = None
                    first = True
                    for (gwt, xsrc) in ((self.gwhi, "hi"), (self.gwlo, "hi"), (self.gwhi, "lo")):
                        for k in range(KD):
                            rhs = (self.xnT[:, k, c * P:(c + 1) * P] if xsrc == "hi"
                                   else loT[:, k * P:(k + 1) * P])
                            last = e.matmul(psL[:], gwt[:, k, :], rhs,
                                            start=first, stop=(not first and k == KD - 1
                                                               and gwt is self.gwhi and xsrc == "lo"))
                            first = False
                    return last
                pr.op("tensor", logit_mms, waits=[("ac", pr.n("ac"))], inc="pe")
                self.pe_hilo = pr.n("pe")
                pr.op("scalar", lambda e: e.copy(lgsb[:], psL[:]),
                      waits=[("pe", pr.n("pe"))], inc="ac")
                pr.op("tensor", lambda e: e.matmul(
                    self.psK[:, 128:128 + NE], lgsb[:], self.identf[0:NE, 0:NE],
                    is_transpose=True, start=True, stop=True),
                    waits=[("ac", pr.n("ac")), ("dv", pr.n("dv"))], inc="pe")
                pr.op("vector", lambda e, cm=cm: e.tensor_copy(
                    self.scores[:, cm, :], self.psK[:, 128:128 + NE]),
                    waits=[("pe", pr.n("pe"))], inc="dv")

        pr.op("tensor", lambda e: e.matmul(
            self.psK[0:NCH, 256:256 + P], self.rstok[:], self.identf[:],
            is_transpose=True, start=True, stop=True),
            waits=[("ac", pr.n("ac"))], inc="pe")
        rs_t = nc.alloc_sbuf_tensor("rs_t", [NCH, P], F32)
        pr.op("vector", lambda e: e.tensor_copy(rs_t[:], self.psK[0:NCH, 256:256 + P]),
              waits=[("pe", pr.n("pe"))], inc="dv")
        pr.op("sync", lambda e: e.dma_start(out=self.rsd, in_=rs_t[:]),
              waits=[("dv", pr.n("dv"))], inc="dS", inck=16)

        self.dump("mrstd", self.mrstd[:])
        self.dump("rstok", self.rstok[:])
        self.dump("xnT", self.xnT[:])
        self.dump("scores_raw", self.scores[:])

        mx = nc.alloc_sbuf_tensor("mx", [P, NCM], F32)
        ex = nc.alloc_sbuf_tensor("ex", [P, NCM, NE], F32)
        sm = nc.alloc_sbuf_tensor("sm", [P, NCM], F32)
        gsc = nc.alloc_sbuf_tensor("gsc", [P, NCM, 4], F32)
        cmp4 = nc.alloc_sbuf_tensor("cmp4", [P, NCM, 4, 3], F32)
        cmp8 = nc.alloc_sbuf_tensor("cmp8", [P, NCM, NE, 7], F32)
        rk4 = nc.alloc_sbuf_tensor("rk4", [P, NCM, 4], F32)
        rk8 = nc.alloc_sbuf_tensor("rk8", [P, NCM, NE], F32)
        msc = nc.alloc_sbuf_tensor("msc", [P, NCM, NE], F32)

        S = self.scores
        dvw = pr.n("dv")
        pr.op("vector", lambda e: e.reduce_max(mx[:, :, None], S[:], axis=AX.X),
              waits=[("dv", dvw)], inc="dv")
        pr.op("vector", lambda e: e.tensor_tensor(
            ex[:], S[:], mx[:, :, None].broadcast_to([P, NCM, NE]), OP.subtract),
            waits=[("dv", pr.n("dv"))], inc="dv")
        pr.op("scalar", lambda e: e.activation(ex[:], ex[:], AF.Exp),
              waits=[("dv", pr.n("dv"))], inc="ac")
        pr.op("vector", lambda e: e.reduce_sum(sm[:, :, None], ex[:], axis=AX.X),
              waits=[("ac", pr.n("ac"))], inc="dv")
        pr.op("vector", lambda e: e.reciprocal(sm[:], sm[:]),
              waits=[("dv", pr.n("dv"))], inc="dv")
        pr.op("vector", lambda e: e.tensor_tensor(
            S[:], ex[:], sm[:, :, None].broadcast_to([P, NCM, NE]), OP.mult),
            waits=[("dv", pr.n("dv"))], inc="dv")
        pr.op("vector", lambda e: e.tensor_tensor(
            gsc[:], S[:, :, 0:NE:2], S[:, :, 1:NE:2], OP.max),
            waits=[("dv", pr.n("dv"))], inc="dv")
        gready = pr.n("dv")
        slot = {g: 0 for g in range(4)}
        for g in range(4):
            for g2 in range(4):
                if g2 == g:
                    continue
                opx = OP.is_ge if g2 < g else OP.is_gt
                j = slot[g]; slot[g] += 1
                pr.op("vector", lambda e, g=g, g2=g2, j=j, opx=opx: e.tensor_tensor(
                    cmp4[:, :, g, j], S_g(gsc, g2), S_g(gsc, g), opx),
                    waits=[("dv", gready)], inc="dv")
        pr.op("vector", lambda e: e.reduce_sum(rk4[:, :, :, None], cmp4[:], axis=AX.X),
              waits=[("dv", pr.n("dv"))], inc="dv")
        pr.op("vector", lambda e: e.tensor_single_scalar(rk4[:], rk4[:], 2.0, OP.is_lt),
              waits=[("dv", pr.n("dv"))], inc="dv")
        pr.op("vector", lambda e: e.tensor_tensor(
            msc[:].rearrange("p c (g r) -> p c g r", r=2),
            S[:].rearrange("p c (g r) -> p c g r", r=2),
            rk4[:, :, :, None].broadcast_to([P, NCM, 4, 2]),
            OP.mult),
            waits=[("dv", pr.n("dv"))], inc="dv")
        mready = pr.n("dv")
        slot8 = {a: 0 for a in range(NE)}
        for a in range(NE):
            for b in range(NE):
                if a == b:
                    continue
                opx = OP.is_ge if b < a else OP.is_gt
                j = slot8[a]; slot8[a] += 1
                pr.op("vector", lambda e, a=a, b=b, j=j, opx=opx: e.tensor_tensor(
                    cmp8[:, :, a, j], msc[:, :, b], msc[:, :, a], opx),
                    waits=[("dv", mready)], inc="dv")
        pr.op("vector", lambda e: e.reduce_sum(rk8[:, :, :, None], cmp8[:], axis=AX.X),
              waits=[("dv", pr.n("dv"))], inc="dv")
        pr.op("vector", lambda e: e.tensor_single_scalar(rk8[:], rk8[:], 2.0, OP.is_lt),
              waits=[("dv", pr.n("dv"))], inc="dv")
        pr.op("vector", lambda e: e.tensor_tensor(self.wmat[:], msc[:], rk8[:], OP.mult),
              waits=[("dv", pr.n("dv"))], inc="dv")
        pr.op("tensor", lambda e: e.matmul(
            self.psK[:, 384:384 + P], self.wmat[:].rearrange("p c e -> p (c e)"),
            self.identf[:], is_transpose=True, start=True, stop=True),
            waits=[("dv", pr.n("dv"))], inc="pe")
        wmT = nc.alloc_sbuf_tensor("wmT", [P, P], F32)
        pr.op("vector", lambda e: e.tensor_copy(wmT[:], self.psK[:, 384:384 + P]),
              waits=[("pe", pr.n("pe"))], inc="dv")
        dvw2 = pr.n("dv")
        for iex in range(NE):
            pr.op("sync", lambda e, iex=iex: e.dma_start(
                out=self.wmd[iex, :].rearrange("(c t) -> c t", t=P),
                in_=wmT[iex::NE, :]),
                waits=[("dv", dvw2)], inc="dS", inck=16)

        self.dump("scores", self.scores[:])
        self.dump("wmat", self.wmat[:])

    def phase_p1(self):
        pr, nc = self.pr, self.nc
        snap = [(s, pr.n(s)) for s in list(pr.semn)]

        self.B_fm = nc.alloc_sbuf_tensor("B_fm", [P, TEXT], BF16)
        self.C_fm = nc.alloc_sbuf_tensor("C_fm", [P, TMAIN], BF16)
        self.dt_fm = nc.alloc_sbuf_tensor("dt_fm", [NH, TEXT], F32)

        pb = [b[:] for b in self.pbank]
        self.pb = pb

        rs_bc = nc.alloc_sbuf_tensor("rs_bc", [P, 5, 512], F32)
        rsrow = nc.alloc_sbuf_tensor("rsrow", [1, 512], F32)
        rsflat = self.rsd.rearrange("c p -> (c p)")
        for tg in range(5):
            pr.op("sync", lambda e, tg=tg: e.dma_start(
                out=rsrow[:], in_=rsflat[tg * 512:(tg + 1) * 512][None, :]),
                waits=snap if tg == 0 else [("pe", pr.n("pe"))], inc="dA", inck=16)
            pr.op("tensor", lambda e: e.matmul(
                pb[7][:], self.onesf[:], rsrow[:], start=True, stop=True),
                waits=[("dA", pr.n("dA"))], inc="pe")
            pr.op("vector", lambda e, tg=tg: e.tensor_copy(rs_bc[:, tg, :], pb[7][:]),
                  waits=[("pe", pr.n("pe"))], inc="dv")
        rsbc_ready = pr.n("dv")

        ZH = 256
        wzf = [self.man(f"wzf{i}", [P, KD, ZH], F32, 80 * 1024 + i * 16 * 1024) for i in range(2)]
        wzb = [self.man(f"wzb{i}", [P, KD, ZH], BF16, 112 * 1024 + i * 8 * 1024) for i in range(2)]
        zst = nc.alloc_sbuf_tensor("zst", [P, 8, 512], BF16)
        w1r = self.w1_h.rearrange("(k p) c -> p k c", p=P)

        zld_free = [snap, snap]
        zcast_free = [0, 0]
        zst_free = [0] * 8
        pb_free_dv = [rsbc_ready] * 8

        for jz in range(6):
            c0 = jz * 512
            for half in range(2):
                pr.op("sync", lambda e, c0=c0, half=half: e.dma_start(
                    out=wzf[half][:], in_=w1r[:, :, c0 + half * ZH:c0 + (half + 1) * ZH]),
                    waits=zld_free[half], inc="dW", inck=16)
                dw = pr.n("dW")
                pr.op("gpsimd", lambda e, half=half: e.tensor_tensor(
                    wzb[half][:], wzf[half][:],
                    self.normw[:, :, None].broadcast_to([P, KD, ZH]), OP.mult),
                    waits=[("dW", dw), ("pe", zcast_free[half])], inc="po")
                zld_free[half] = [("po", pr.n("po"))]
            po_ready = pr.n("po")

            for tcg in range(2):
                for half in range(2):
                    def zmm(e, tcg=tcg, half=half):
                        last = None
                        for k in range(KD):
                            for tc in range(8):
                                t0 = WARM + tcg * 1024 + tc * P
                                last = e.matmul(
                                    pb[tc][:, half * ZH:(half + 1) * ZH],
                                    self.xnT[:, k, t0:t0 + P],
                                    wzb[half][:, k, :],
                                    start=(half == 0 and k == 0),
                                    stop=(half == 1 and k == KD - 1))
                        return last
                    pr.op("tensor", zmm,
                          waits=[("po", po_ready)] + [("dv", max(pb_free_dv))],
                          inc="pe")
                zcast_free[0] = zcast_free[1] = pr.n("pe")
                pe_done = pr.n("pe")
                for tc in range(8):
                    cidx = NCW + tcg * 8 + tc
                    pr.op("vector", lambda e, tc=tc, cidx=cidx: e.tensor_scalar_mul(
                        zst[:, tc, :], pb[tc][:], self.rstok[:, cidx:cidx + 1]),
                        waits=[("pe", pe_done), ("dS", zst_free[tc])], inc="dv")
                    pb_free_dv[tc] = pr.n("dv")
                    pr.op("scalar", lambda e, tc=tc, tcg=tcg, jz=jz: e.dma_start(
                        out=self.zd[(tcg * 8 + tc) * P:(tcg * 8 + tc + 1) * P,
                                    jz * 512:(jz + 1) * 512],
                        in_=zst[:, tc, :]),
                        waits=[("dv", pr.n("dv"))], inc="dS", inck=16)
                    zst_free[tc] = pr.n("dS")

        wjf = [self.man(f"wjf{i}", [P, KD, P], F32, 80 * 1024 + i * 8 * 1024) for i in range(2)]
        wjb = [self.man(f"wjb{i}", [P, KD, P], BF16, 96 * 1024 + i * 4 * 1024) for i in range(2)]
        xcst = nc.alloc_sbuf_tensor("xcst", [P, 2, 512], BF16)
        z_done = [("pe", pr.n("pe")), ("po", pr.n("po"))]

        jobs = [("xc", 3072 + 128 * j, P, j) for j in range(JIN)]
        jobs += [("B", 6144, P, 0), ("C", 6272, P, 0), ("dt", 6400, NH, 0)]
        jld_free = [z_done, z_done]
        jcast_free = [0, 0]
        xc_free = [0, 0]
        fm_pb_free = [max(pb_free_dv)] * 5

        for ji, (kind, c0, w, j) in enumerate(jobs):
            par = ji % 2
            pr.op("sync", lambda e, c0=c0, w=w, par=par: e.dma_start(
                out=wjf[par][:, :, 0:w], in_=w1r[:, :, c0:c0 + w]),
                waits=jld_free[par], inc="dW", inck=16)
            dw = pr.n("dW")
            pr.op("gpsimd", lambda e, par=par, w=w: e.tensor_tensor(
                wjb[par][:, :, 0:w], wjf[par][:, :, 0:w],
                self.normw[:, :, None].broadcast_to([P, KD, w]), OP.mult),
                waits=[("dW", dw), ("pe", jcast_free[par])], inc="po")
            jld_free[par] = [("po", pr.n("po"))]
            po_ready = pr.n("po")

            tgs = [1, 2, 3, 4] if kind == "C" else [0, 1, 2, 3, 4]
            for tgi, tg in enumerate(tgs):
                def jmm(e, par=par, w=w, tg=tg, tgi=tgi):
                    last = None
                    for k in range(KD):
                        last = e.matmul(pb[tgi][0:w, :],
                                        wjb[par][:, k, 0:w],
                                        self.xnT[:, k, tg * 512:(tg + 1) * 512],
                                        start=(k == 0), stop=(k == KD - 1))
                    return last
                pr.op("tensor", jmm,
                      waits=[("po", po_ready), ("dv", fm_pb_free[tgi])], inc="pe")
            jcast_free[par] = pr.n("pe")
            pe_done = pr.n("pe")

            for tgi, tg in enumerate(tgs):
                t0 = tg * 512
                if kind == "xc":
                    spar = tgi % 2
                    pr.op("vector", lambda e, tgi=tgi, tg=tg, spar=spar: e.tensor_tensor(
                        xcst[:, spar, :], pb[tgi][:], rs_bc[:, tg, :], OP.mult),
                        waits=[("pe", pe_done), ("dS", xc_free[spar])], inc="dv")
                    fm_pb_free[tgi] = pr.n("dv")
                    pr.op("scalar", lambda e, j=j, t0=t0, spar=spar: e.dma_start(
                        out=self.xcd[j, :, t0:t0 + 512], in_=xcst[:, spar, :]),
                        waits=[("dv", pr.n("dv"))], inc="dS", inck=16)
                    xc_free[spar] = pr.n("dS")
                elif kind == "B":
                    pr.op("vector", lambda e, tgi=tgi, tg=tg, t0=t0: e.tensor_tensor(
                        self.B_fm[:, t0:t0 + 512], pb[tgi][:], rs_bc[:, tg, :], OP.mult),
                        waits=[("pe", pe_done)], inc="dv")
                    fm_pb_free[tgi] = pr.n("dv")
                elif kind == "C":
                    pr.op("vector", lambda e, tgi=tgi, tg=tg, t0=t0: e.tensor_tensor(
                        self.C_fm[:, t0 - WARM:t0 - WARM + 512], pb[tgi][:],
                        rs_bc[:, tg, :], OP.mult),
                        waits=[("pe", pe_done)], inc="dv")
                    fm_pb_free[tgi] = pr.n("dv")
                else:
                    pr.op("vector", lambda e, tgi=tgi, tg=tg, t0=t0: e.tensor_tensor(
                        self.dt_fm[:, t0:t0 + 512], pb[tgi][0:NH, :],
                        rs_bc[0:NH, tg, :], OP.mult),
                        waits=[("pe", pe_done)], inc="dv")
                    fm_pb_free[tgi] = pr.n("dv")

        self.dump("xnT_after", self.xnT[:])
        self.dump("B_fm", self.B_fm[:])
        self.dump("C_fm", self.C_fm[:])
        self.dump("dt_fm", self.dt_fm[:])

    def phase_p3a(self):
        pr, nc = self.pr, self.nc
        pb = self.pb
        p1_done = [("pe", pr.n("pe")), ("po", pr.n("po")), ("dv", pr.n("dv")),
                   ("ac", pr.n("ac"))]

        w3f = [self.man(f"w3f{i}", [P, KD, P], F32, 80 * 1024 + i * 8 * 1024) for i in range(2)]
        w3b = [self.man(f"w3b{i}", [P, KD, P], BF16, 96 * 1024 + i * 4 * 1024) for i in range(2)]
        sg = self.man("sg_a", [P, 4, 512], BF16, 104 * 1024)
        uw = self.man("uw_a", [P, 512], BF16, 108 * 1024)
        ast = [self.man(f"ast{i}", [P, 4, 512], BF16, 110 * 1024 + i * 4 * 1024) for i in range(2)]
        wbc = self.man("wbc", [P, D], F32, 118 * 1024)
        wmrow = self.man("wmrow", [1, D], F32, 126 * 1024)
        w3r = self.w3_h.rearrange("(k p) c -> p k c", p=P)

        ld_free = [p1_done, p1_done]
        cast_free = [0, 0]
        gbank_free = [max(0, 0)] * 4
        gbank_free_ac = [0] * 4
        ubank_free = [0] * 4
        ast_free = [0, 0]
        wbc_free = 0
        sg_free = 0

        gbank_free_dv0 = pr.n("dv")
        self.sg_free_dv = 0

        jobs = []
        for n in range(NE):
            jobs.append(("wb", n, 0))
            for f in range(11):
                jobs.append(("g", n, f))
                jobs.append(("u", n, f))
        for f in range(22):
            jobs.append(("g", NE, f))
            jobs.append(("u", NE, f))

        ji = 0
        for kind, n, f in jobs:
            if kind == "wb":
                pr.op("sync", lambda e, n=n: e.dma_start(
                    out=wmrow[:], in_=self.wmd[n, :][None, :]),
                    waits=p1_done if n == 0 else [("dv", wbc_free)], inc="dA", inck=16)
                da = pr.n("dA")
                for tg in range(4):
                    pr.op("tensor", lambda e, tg=tg: e.matmul(
                        pb[4 + tg][:], self.onesf[:], wmrow[:, tg * 512:(tg + 1) * 512],
                        start=True, stop=True),
                        waits=[("dA", da), ("dv", ubank_free[tg]),
                               ("po", sg_free)] + (p1_done if n == 0 else []),
                        inc="pe")
                    pr.op("vector", lambda e, tg=tg: e.tensor_copy(
                        wbc[:, tg * 512:(tg + 1) * 512], pb[4 + tg][:]),
                        waits=[("pe", pr.n("pe"))], inc="dv")
                    ubank_free[tg] = pr.n("dv")
                wbc_free = pr.n("dv")
                continue

            par = ji % 2
            ji += 1
            shared = (n == NE)
            if kind == "g":
                c0 = (22528 + f * P) if shared else (n * 2 * FE + f * P)
            else:
                c0 = (22528 + FS + f * P) if shared else (n * 2 * FE + FE + f * P)

            pr.op("sync", lambda e, c0=c0, par=par: e.dma_start(
                out=w3f[par][:], in_=w3r[:, :, c0:c0 + P]),
                waits=ld_free[par], inc="dW", inck=16)
            dw = pr.n("dW")
            pr.op("gpsimd", lambda e, par=par: e.tensor_copy(w3b[par][:], w3f[par][:]),
                  waits=[("dW", dw), ("pe", cast_free[par])], inc="po")
            ld_free[par] = [("po", pr.n("po"))]
            po_ready = pr.n("po")

            for tg in range(4):
                bank = tg if kind == "g" else 4 + tg

                def mm(e, par=par, tg=tg, bank=bank):
                    last = None
                    t0 = WARM + tg * 512
                    for k in range(KD):
                        last = e.matmul(pb[bank][:], w3b[par][:, k, :],
                                        self.xnT[:, k, t0:t0 + 512],
                                        start=(k == 0), stop=(k == KD - 1))
                    return last
                if kind == "g":
                    w_ = [("po", po_ready), ("ac", gbank_free_ac[tg]),
                          ("po", sg_free), ("dv", gbank_free_dv0)]
                else:
                    w_ = [("po", po_ready), ("dv", ubank_free[tg])]
                pr.op("tensor", mm, waits=w_, inc="pe")
            cast_free[par] = pr.n("pe")
            pe_done = pr.n("pe")

            if kind == "g":
                for tg in range(4):
                    pr.op("scalar", lambda e, tg=tg: e.activation(
                        sg[:, tg, :], pb[tg][:], AF.Silu),
                        waits=[("pe", pe_done), ("po", sg_free),
                               ("dv", self.sg_free_dv)], inc="ac")
                    gbank_free_ac[tg] = pr.n("ac")
            else:
                row0 = (11264 + f * P) if shared else (n * FE + f * P)
                apar = (ji // 2) % 2
                for tg in range(4):
                    if shared:
                        pr.op("vector", lambda e, tg=tg, apar=apar: e.tensor_tensor(
                            ast[apar][:, tg, :], sg[:, tg, :], pb[4 + tg][:], OP.mult),
                            waits=[("pe", pe_done), ("ac", gbank_free_ac[tg]),
                                   ("dS", ast_free[apar])], inc="dv")
                        ubank_free[tg] = pr.n("dv")
                    else:
                        pr.op("vector", lambda e, tg=tg: e.tensor_tensor(
                            uw[:], pb[4 + tg][:], wbc[:, tg * 512:(tg + 1) * 512],
                            OP.mult),
                            waits=[("pe", pe_done), ("po", pr.n("po")),
                                   ("ac", gbank_free_ac[tg])], inc="dv")
                        ubank_free[tg] = pr.n("dv")
                        pr.op("gpsimd", lambda e, tg=tg, apar=apar: e.tensor_tensor(
                            ast[apar][:, tg, :], sg[:, tg, :], uw[:], OP.mult),
                            waits=[("dv", pr.n("dv")), ("dS", ast_free[apar])], inc="po")
                if shared:
                    self.sg_free_dv = pr.n("dv")
                else:
                    sg_free = pr.n("po")
                wdep = ("dv", pr.n("dv")) if shared else ("po", pr.n("po"))
                pr.op("scalar", lambda e, row0=row0, apar=apar: e.dma_start(
                    out=self.ad[row0:row0 + P, :], in_=ast[apar][:].rearrange("p g t -> p (g t)")),
                    waits=[wdep], inc="dS", inck=16)
                ast_free[apar] = pr.n("dS")

        self.dump("ad_e0", self.ad[0:256, :])
        self.dump("ad_sh", self.ad[11264:11520, :])

    def phase_p2(self):
        pr, nc = self.pr, self.nc
        pball = [b[:] for b in self.pbank]
        p3_done = [(s, pr.n(s)) for s in list(pr.semn)]

        import os as _os
        _nal = int(_os.environ.get("P2_ALLOC", 99))
        _ac = [0]
        _orig_man = self.man
        def _man(name, shape, dt, off):
            _ac[0] += 1
            if _ac[0] > _nal:
                return _orig_man(name, shape, dt, 160 * 1024 - 16 * 1024 + 0) if False else None
            return _orig_man(name, shape, dt, off)
        self.man = _man
        h_sb = self.man("h_sb", [P, DIN], F32, 0)
        h_bf = self.man("h_bf", [P, DIN], BF16, 12 * 1024)
        y = self.man("y_sb", [P, DIN], F32, 18 * 1024)
        h0sb = self.man("h0sb", [P, DIN], F32, 18 * 1024)
        Xsil = [self.man(f"Xsil{i}", [P, JIN, P], BF16, 30 * 1024 + i * 6 * 1024)
                for i in range(2)]
        Xtok = [self.man(f"Xtok{i}", [P, DIN], BF16, 42 * 1024 + i * 6 * 1024)
                for i in range(2)]
        conv = self.man("conv", [P, JIN, P], F32, 54 * 1024)
        zld = [self.man(f"zld{i}", [P, DIN], BF16, 66 * 1024 + i * 6 * 1024)
               for i in range(2)]
        xct = [self.man(f"xct{i}", [P, JIN, 131], BF16, 78 * 1024 + i * 6656)
               for i in range(2)]
        tL8 = [self.man(f"tL8_{i}", [2, 8 * P], F32, 91 * 1024 + i * 4096)
               for i in range(2)]
        tR8 = [self.man(f"tR8_{i}", [2, 8 * P], F32, 99 * 1024 + i * 4096)
               for i in range(2)]
        lfm = self.man("lfm", [NH, TEXT], F32, 107 * 1024)
        G8 = self.man("G8", [P, 8, P], BF16, 117 * 1024)
        Wt8 = self.man("Wt8", [P, 8, P], BF16, 119 * 1024)
        sm2 = self.man("sm2", [P, 512], F32, 121 * 1024)
        G_sb = self.man("G_sb", [P, P], BF16, 123 * 1024)
        Btd8 = self.man("Btd8", [P, 8, P], BF16, 123 * 1024 + 256)
        B_tok = self.man("B_tok", [P, P], BF16, 125 * 1024 + 512)
        dca = self.man("dca", [112, P], F32, 125 * 1024 + 768)
        cscr = self.man("cscr", [P, JIN // 2, P], F32, 134 * 1024)

        self.man = _orig_man
        if any(v is None for v in [h_sb, h_bf, y, h0sb, conv, lfm, G8, Wt8, sm2,
                                   G_sb, Btd8, B_tok, dca, cscr] + Xsil + Xtok + zld
               + xct + tL8 + tR8):
            return
        scr1 = conv[:, :, :].rearrange("p j t -> p (j t)")[0:NH, 0:TEXT]
        if int(_os.environ.get("P2_AUTOS", 1)) < 1:
            return
        ssqy = self.man("ssqy", [P, 1], F32, 140 * 1024)
        yt1 = self.man("yt1", [P, 1], F32, 140 * 1024 + 32)
        ends = self.man("ends", [NH, NCH], F32, 140 * 1024 + 64)
        prev = self.man("prev", [NH, NCH], F32, 140 * 1024 + 160)
        dxrow = self.man("dxrow", [1, NH], F32, 140 * 1024 + 256)
        dt = self.dt_fm

        import os
        PRE = int(os.environ.get("P2_PRE", 9))
        if PRE < 1:
            return
        pr.op("sync", lambda e: e.dma_start(out=h0sb[:], in_=self.h0_h),
              waits=p3_done, inc="dA", inck=16)
        pr.op("gpsimd", lambda e: e.memset(h_sb[:], 0.0), waits=p3_done, inc="po")
        pr.op("gpsimd", lambda e: e.memset(h_bf[:], 0.0), inc="po")
        if PRE < 2:
            return
        pr.op("vector", lambda e: e.tensor_scalar_add(dt[:], dt[:], self.abd[:, 1:2]),
              waits=p3_done, inc="dv")
        pr.op("scalar", lambda e: e.activation(scr1, dt[:], AF.Abs),
              waits=[("dv", pr.n("dv"))] + p3_done, inc="ac")
        pr.op("scalar", lambda e: e.activation(scr1, scr1, AF.Exp, scale=-1.0),
              waits=[("ac", pr.n("ac"))], inc="ac")
        pr.op("scalar", lambda e: e.activation(scr1, scr1, AF.Ln, bias=self.cc[0:NH, 1:2]),
              waits=[("ac", pr.n("ac"))], inc="ac")
        pr.op("vector", lambda e: e.tensor_scalar_max(dt[:], dt[:], 0.0),
              waits=[("ac", pr.n("ac") - 2)], inc="dv")
        pr.op("vector", lambda e: e.tensor_tensor(dt[:], dt[:], scr1, OP.add),
              waits=[("ac", pr.n("ac")), ("dv", pr.n("dv"))], inc="dv")
        if PRE < 3:
            return
        pr.op("vector", lambda e: e.tensor_scalar_mul(scr1, dt[:], self.negA[:]),
              waits=[("dv", pr.n("dv"))], inc="dv")
        pr.op("vector", lambda e: e.tensor_tensor_scan(
            lfm[:], scr1, self.cc[0:NH, 2:3].broadcast_to([NH, TEXT]), 0.0,
            OP.add, OP.add),
            waits=[("dv", pr.n("dv"))], inc="dv")
        lfm_dv = pr.n("dv")
        if PRE < 4:
            return
        pr.op("vector", lambda e: e.tensor_copy(ends[:], lfm[:, P - 1::P]),
              waits=[("dv", lfm_dv)], inc="dv")
        pr.op("gpsimd", lambda e: e.memset(prev[:, 0:1], 0.0), inc="po")
        pr.op("vector", lambda e: e.tensor_copy(prev[:, 1:NCH], ends[:, 0:NCH - 1]),
              waits=[("dv", pr.n("dv"))], inc="dv")
        pr.op("vector", lambda e: e.tensor_tensor(
            dca[0:NH, 0:NCH], ends[:], prev[:], OP.subtract),
            waits=[("dv", pr.n("dv")), ("po", pr.n("po"))] + p3_done, inc="dv")
        pr.op("scalar", lambda e: e.activation(dca[0:NH, 0:NCH], dca[0:NH, 0:NCH], AF.Exp),
              waits=[("dv", pr.n("dv"))], inc="ac")
        if PRE < 5:
            return
        pr.op("tensor", lambda e: e.matmul(
            pball[5][0:NCH, 0:NH], dca[0:NH, 0:NCH], self.identf[0:NH, 0:NH],
            is_transpose=True, start=True, stop=True),
            waits=[("ac", pr.n("ac"))] + p3_done, inc="pe")
        pr.op("vector", lambda e: e.tensor_copy(sm2[0:NCH, 176:176 + NH],
                                                pball[5][0:NCH, 0:NH]),
              waits=[("pe", pr.n("pe"))], inc="dv")
        pr.op("scalar", lambda e: e.dma_start(out=self.dxd, in_=sm2[0:NCH, 176:176 + NH]),
              waits=[("dv", pr.n("dv"))], inc="dS", inck=16)
        dxd_ds = pr.n("dS")
        if PRE < 6:
            return
        pr.op("scalar", lambda e: e.dma_start(out=self.ld, in_=lfm[:]),
              waits=[("dv", lfm_dv)], inc="dS", inck=16)
        ld_ds = pr.n("dS")
        for i in range(2):
            pr.op("gpsimd", lambda e, i=i: e.memset(tL8[i][:], 1.0),
                  waits=p3_done, inc="po")
            pr.op("gpsimd", lambda e, i=i: e.memset(tR8[i][:], -1.0), inc="po")
        prelude_po = pr.n("po")
        ps5_state = {"pe": pr.n("pe"), "dv": pr.n("dv"), "ac": pr.n("ac")}

        self.dump("lfm", lfm[:])
        self.dump("dtsp", dt[:])

        bE = [pball[0], pball[1]]
        bYl, bYI, bS = pball[2], pball[3], pball[4]
        b5 = pball[5]
        b5b = b5.bitcast(BF16)
        bX = [pball[6].bitcast(BF16), pball[7].bitcast(BF16)]

        st = {
            "xct_rd": [p3_done, p3_done],
            "zld_rd": [[], []],
            "Xsil_rd": [p3_done, p3_done],
            "Xtok_rd": [[], []],
            "conv_rd": [("ac", 0)],
            "tileL_pe": [0, 0],
            "dxrow_pe": 0,
            "ebank_rd": [0, 0],
            "y_ac": 0,
            "ybank_dv": 0,
            "psX_ac": [0, 0],
            "hbf_pe": 0,
            "G_po": 0,
        }

        import os
        nch_lim = int(os.environ.get('P2_CHUNKS', NCH))
        MLVL = int(os.environ.get('P2_MLVL', 9))
        for c in range(nch_lim):
            par = c % 2
            main = c >= NCW
            cm = c - NCW
            zpar = cm % 2 if main else 0
            t0 = c * P

            if c == 0:
                pr.op("sync", lambda e: e.dma_start(
                    out=xct[0][:, :, 3:131],
                    in_=self.xcd.rearrange("j p t -> p j t")[:, :, 0:P]),
                    waits=p3_done, inc="dA", inck=16)
                pr.op("gpsimd", lambda e: e.memset(xct[0][:, :, 0:3], 0.0),
                      waits=p3_done, inc="po")
            else:
                pr.op("sync", lambda e, par=par, t0=t0: e.dma_start(
                    out=xct[par][:],
                    in_=self.xcd.rearrange("j p t -> p j t")[:, :, t0 - 3:t0 + P]),
                    waits=st["xct_rd"][par], inc="dA", inck=16)
            xct_da = pr.n("dA")
            if main:
                pr.op("sync", lambda e, zpar=zpar, cm=cm: e.dma_start(
                    out=zld[zpar][:], in_=self.zd[cm * P:(cm + 1) * P, :]),
                    waits=st["zld_rd"][zpar] + (p3_done if cm < 2 else []),
                    inc="dA", inck=16)
                zld_da = pr.n("dA")
            tl_da = pr.n("dA")
            pr.op("sync", lambda e, c=c: e.dma_start(
                out=dxrow[:], in_=self.dxd[c, :][None, :]),
                waits=[("dS", dxd_ds), ("pe", st["dxrow_pe"])], inc="dA", inck=16)
            dx_da = pr.n("dA")

            blend_dv = 0
            if c == NCW:
                pr.op("vector", lambda e, par=par: e.tensor_scalar_mul(
                    xct[par][:, :, 0:3], xct[par][:, :, 0:3], self.mv[:, 0:1]),
                    waits=[("dA", xct_da)], inc="dv")
                pr.op("vector", lambda e, par=par: e.scalar_tensor_tensor(
                    xct[par][:, :, 0:3], self.csfm[:], self.mv[:, 1:2],
                    xct[par][:, :, 0:3], OP.mult, OP.add),
                    waits=[("dv", pr.n("dv"))], inc="dv")
                blend_dv = pr.n("dv")

            cw, cb = self.convw, self.convb
            JH = JIN // 2
            for half in range(2):
                jl = slice(half * JH, (half + 1) * JH)
                waits0 = [("dA", xct_da)] + st["conv_rd"] + \
                    ([("dv", blend_dv)] if blend_dv else []) + \
                    (p3_done if c == 0 else [])
                pr.op("vector", lambda e, par=par, jl=jl: e.tensor_tensor(
                    conv[:, jl, :], xct[par][:, jl, 0:P],
                    cw[:, jl, 0:1].broadcast_to([P, JH, P]), OP.mult),
                    waits=waits0, inc="dv")
                pr.op("gpsimd", lambda e, par=par, jl=jl: e.tensor_tensor(
                    cscr[:, 0:JH, :], xct[par][:, jl, 1:1 + P],
                    cw[:, jl, 1:2].broadcast_to([P, JH, P]), OP.mult),
                    waits=waits0 + [("dv", pr.n("dv") - 1)], inc="po")
                pr.op("vector", lambda e, jl=jl: e.tensor_tensor(
                    conv[:, jl, :], conv[:, jl, :], cscr[:, 0:JH, :], OP.add),
                    waits=[("po", pr.n("po")), ("dv", pr.n("dv"))], inc="dv")
                pr.op("gpsimd", lambda e, par=par, jl=jl: e.tensor_tensor(
                    cscr[:, 0:JH, :], xct[par][:, jl, 2:2 + P],
                    cw[:, jl, 2:3].broadcast_to([P, JH, P]), OP.mult),
                    waits=[("dv", pr.n("dv"))], inc="po")
                pr.op("vector", lambda e, jl=jl: e.tensor_tensor(
                    conv[:, jl, :], conv[:, jl, :], cscr[:, 0:JH, :], OP.add),
                    waits=[("po", pr.n("po")), ("dv", pr.n("dv"))], inc="dv")
                pr.op("gpsimd", lambda e, par=par, jl=jl: e.tensor_tensor(
                    cscr[:, 0:JH, :], xct[par][:, jl, 3:3 + P],
                    cw[:, jl, 3:4].broadcast_to([P, JH, P]), OP.mult),
                    waits=[("dv", pr.n("dv"))], inc="po")
                pr.op("vector", lambda e, jl=jl: e.tensor_tensor(
                    conv[:, jl, :], conv[:, jl, :], cscr[:, 0:JH, :], OP.add),
                    waits=[("po", pr.n("po")), ("dv", pr.n("dv"))], inc="dv")
                pr.op("vector", lambda e, jl=jl: e.tensor_tensor(
                    conv[:, jl, :], conv[:, jl, :],
                    cb[:, jl, None].broadcast_to([P, JH, P]), OP.add),
                    waits=[("dv", pr.n("dv"))], inc="dv")
            conv_dv = pr.n("dv")
            conv_po = pr.n("po")
            st["xct_rd"][par] = [("dv", conv_dv), ("po", pr.n("po"))]

            pr.op("scalar", lambda e, par=par: e.activation(
                Xsil[par][:], conv[:], AF.Silu),
                waits=[("dv", conv_dv), ("po", conv_po)] + st["Xsil_rd"][par],
                inc="ac")
            st["conv_rd"] = [("ac", pr.n("ac"))]
            xsil_ac = pr.n("ac")
            if main:
                pr.op("scalar", lambda e, zpar=zpar: e.activation(
                    zld[zpar][:], zld[zpar][:], AF.Silu),
                    waits=[("dA", zld_da)], inc="ac")
                siluz_ac = pr.n("ac")

            for w in range(3):
                bx = w % 2

                def xtp(e, par=par, w=w, bx=bx):
                    last = None
                    for i in range(8):
                        j = w * 8 + i
                        last = e.matmul(bX[bx][:, i * P:(i + 1) * P],
                                        Xsil[par][:, j, :], self.identb[:],
                                        is_transpose=True, start=(i == 0), stop=(i == 7))
                    return last
                pr.op("tensor", xtp,
                      waits=[("ac", max(xsil_ac, st["psX_ac"][bx]))], inc="pe")
                pe_h = pr.n("pe")

                def xev(e, par=par, w=w, bx=bx):
                    last = None
                    for i in range(8):
                        j = w * 8 + i
                        last = e.copy(Xtok[par][:, j * P:(j + 1) * P],
                                      bX[bx][:, i * P:(i + 1) * P])
                    return last
                pr.op("scalar", xev,
                      waits=[("pe", pe_h)] + st["Xtok_rd"][par], inc="ac")
                st["psX_ac"][bx] = pr.n("ac")
            xtok_ac = pr.n("ac")
            st["Xsil_rd"][par] = [("pe", pr.n("pe"))]

            pr.op("vector", lambda e, c=c, t0=t0: e.tensor_scalar(
                dca[0:NH, :], lfm[:, t0:t0 + P], prev[:, c:c + 1], None, OP.subtract),
                waits=[("dv", 0), ("pe", ps5_state["pe"])], inc="dv")
            pr.op("vector", lambda e, c=c, t0=t0: e.tensor_scalar(
                dca[64:64 + NH, :], lfm[:, t0:t0 + P], ends[:, c:c + 1], -1.0,
                OP.subtract, OP.mult),
                waits=[("dv", pr.n("dv") - 1)], inc="dv")
            dca_dv = pr.n("dv")
            pr.op("scalar", lambda e: e.activation(dca[0:NH, :], dca[0:NH, :], AF.Exp),
                  waits=[("dv", dca_dv)], inc="ac")
            pr.op("scalar", lambda e: e.activation(dca[64:64 + NH, :], dca[64:64 + NH, :],
                                                   AF.Exp),
                  waits=[("ac", pr.n("ac"))], inc="ac")
            dca_ac = pr.n("ac")

            def b5ops(e, c=c, main=main, cm=cm, t0=t0):
                last = e.matmul(b5[:, 0:NH], self.onesf[:], dxrow[:],
                                start=True, stop=False)
                last = e.matmul(b5[:, 64:176], dca[:], self.identf[0:112, 0:112],
                                is_transpose=True, start=False, stop=False)
                last = e.matmul(b5b[:, 640:768], self.B_fm[:, t0:t0 + P],
                                self.identb[:], is_transpose=True,
                                start=False, stop=True)
                return last
            pr.op("tensor", b5ops,
                  waits=[("dA", dx_da), ("ac", dca_ac),
                         ("dv", ps5_state["dv"]), ("ac", ps5_state["ac"])], inc="pe")
            st["dxrow_pe"] = pr.n("pe")
            b5_pe = pr.n("pe")
            pr.op("vector", lambda e: e.tensor_copy(sm2[:, 0:112], b5[:, 64:176]),
                  waits=[("pe", b5_pe)], inc="dv")
            pr.op("vector", lambda e: e.tensor_copy(sm2[:, 128:176], b5[:, 0:NH]),
                  waits=[("dv", pr.n("dv") - 1)], inc="dv")
            pr.op("vector", lambda e: e.tensor_copy(B_tok[:], b5b[:, 640:768]),
                  waits=[("dv", pr.n("dv") - 1)], inc="dv")
            if main and MLVL >= 3:
                pr.op("tensor", lambda e, t0=t0, cm=cm: e.matmul(
                    bE[0][:, 0:P], self.B_fm[:, t0:t0 + P],
                    self.C_fm[:, cm * P:(cm + 1) * P], start=True, stop=True),
                    waits=[("ac", max(st["ebank_rd"]))], inc="pe")
                pr.op("scalar", lambda e: e.copy(G_sb[:], bE[0][:, 0:P]),
                      waits=[("pe", pr.n("pe")), ("po", st["G_po"])], inc="ac")
                gsb_ac = pr.n("ac")
                st["ebank_rd"] = [pr.n("ac"), st["ebank_rd"][1]]
            b5ev_dv = pr.n("dv")
            ps5_state = {"pe": b5_pe, "dv": b5ev_dv, "ac": pr.n("ac")}

            for w in range(6):
                hh0 = w * 8
                if main and MLVL >= 2:
                    wpar = w % 2
                    lsrc = self.ld[:, t0:t0 + P][hh0:hh0 + 8][None, :, :]
                    dl = tL8[wpar][0:1, :].rearrange("o (h s) -> o h s", s=P)
                    drr = tR8[wpar][1:2, :].rearrange("o (h s) -> o h s", s=P)
                    pr.op("sync", lambda e, lsrc=lsrc, dl=dl: e.dma_start(
                        out=dl, in_=lsrc),
                        waits=[("dS", ld_ds), ("pe", st["tileL_pe"][wpar]),
                               ("po", prelude_po)], inc="dA", inck=16)
                    pr.op("sync", lambda e, lsrc=lsrc, drr=drr: e.dma_start(
                        out=drr, in_=lsrc), inc="dA", inck=16)
                    tl8_da = pr.n("dA")

                    def emms(e, hh0=hh0, wpar=wpar):
                        last = None
                        for i in range(8):
                            bank = bE[i // 4]
                            sl = i % 4
                            last = e.matmul(bank[:, sl * P:(sl + 1) * P],
                                            tL8[wpar][0:2, i * P:(i + 1) * P],
                                            tR8[wpar][0:2, i * P:(i + 1) * P],
                                            start=(sl == 0), stop=(sl == 3))
                        return last
                    pr.op("tensor", emms,
                          waits=[("dA", tl8_da), ("ac", max(st["ebank_rd"]))], inc="pe")
                    st["tileL_pe"][wpar] = pr.n("pe")
                    e_pe = pr.n("pe")

                    def gexp(e, hh0=hh0):
                        last = None
                        for i in range(8):
                            bank = bE[i // 4]
                            sl = i % 4
                            last = e.activation(G8[:, i, :],
                                                bank[:, sl * P:(sl + 1) * P], AF.Exp)
                        return last
                    pr.op("scalar", gexp, waits=[("pe", e_pe)], inc="ac")
                    st["ebank_rd"] = [pr.n("ac"), pr.n("ac")]

                    def gmask(e):
                        last = None
                        for i in range(8):
                            last = e.affine_select(
                                out=G8[:, i, :], in_=G8[:, i, :],
                                compare_op=OP.is_ge, fill=0.0, base=0,
                                pattern=[[1, P]], channel_multiplier=-1)
                        return last
                    pr.op("gpsimd", gmask, waits=[("ac", pr.n("ac"))], inc="po")

                    if MLVL >= 4:
                        def wmul(e):
                            last = None
                            for i in range(8):
                                last = e.tensor_tensor(Wt8[:, i, :], G8[:, i, :],
                                                       G_sb[:], OP.mult)
                            return last
                        pr.op("gpsimd", wmul,
                              waits=[("po", pr.n("po")), ("ac", gsb_ac)], inc="po")
                        wt_po = pr.n("po")
                        st["G_po"] = wt_po

                def btd(e, hh0=hh0):
                    last = None
                    for i in range(8):
                        h = hh0 + i
                        last = e.tensor_scalar_mul(Btd8[:, i, :], B_tok[:],
                                                   sm2[:, 64 + h:65 + h])
                    return last
                pr.op("vector", btd, waits=[("dv", b5ev_dv), ("pe", pr.n("pe"))],
                      inc="dv")
                btd_dv = pr.n("dv")

                def wavemm(e, hh0=hh0, main=main, cm=cm, par=par):
                    last = None
                    if main and MLVL >= 5:
                        for i in range(8):
                            h = hh0 + i
                            last = e.matmul(bYl[:, i * DH:(i + 1) * DH],
                                            Wt8[:, i, :],
                                            Xtok[par][:, h * DH:(h + 1) * DH],
                                            start=(i == 0), stop=(i == 7))
                        for i in range(8):
                            h = hh0 + i
                            last = e.matmul(bYI[:, i * DH:(i + 1) * DH],
                                            self.C_fm[:, cm * P:(cm + 1) * P],
                                            h_bf[:, h * DH:(h + 1) * DH],
                                            start=(i == 0), stop=(i == 7))
                    for i in range(8):
                        h = hh0 + i
                        last = e.matmul(bS[:, i * DH:(i + 1) * DH],
                                        Btd8[:, i, :],
                                        Xtok[par][:, h * DH:(h + 1) * DH],
                                        start=(i == 0), stop=(i == 7))
                    return last
                mmw = [("dv", max(btd_dv, st["ybank_dv"])), ("ac", xtok_ac)]
                if main and MLVL >= 5:
                    mmw.append(("po", wt_po))
                pr.op("tensor", wavemm, waits=mmw, inc="pe")
                wave_pe = pr.n("pe")
                st["hbf_pe"] = wave_pe

                def stts(e, hh0=hh0, main=main, par=par):
                    last = None
                    for i in range(8):
                        h = hh0 + i
                        if main and MLVL >= 6:
                            last = e.scalar_tensor_tensor(
                                y[:, h * DH:(h + 1) * DH],
                                Xtok[par][:, h * DH:(h + 1) * DH],
                                self.Dbc[:, h:h + 1],
                                bYl[:, i * DH:(i + 1) * DH], OP.mult, OP.add)
                            last = e.scalar_tensor_tensor(
                                y[:, h * DH:(h + 1) * DH],
                                bYI[:, i * DH:(i + 1) * DH],
                                sm2[:, h:h + 1],
                                y[:, h * DH:(h + 1) * DH], OP.mult, OP.add)
                        last = e.scalar_tensor_tensor(
                            h_sb[:, h * DH:(h + 1) * DH],
                            h_sb[:, h * DH:(h + 1) * DH],
                            sm2[:, 128 + h:129 + h],
                            bS[:, i * DH:(i + 1) * DH], OP.mult, OP.add)
                    return last
                pr.op("vector", stts,
                      waits=[("pe", wave_pe), ("ac", st["y_ac"] if main and w == 0 else 0),
                             ("po", 0)], inc="dv")
                st["ybank_dv"] = pr.n("dv")

                def hbfc(e, hh0=hh0):
                    last = None
                    for i in range(8):
                        h = hh0 + i
                        last = e.tensor_copy(h_bf[:, h * DH:(h + 1) * DH],
                                             h_sb[:, h * DH:(h + 1) * DH])
                    return last
                pr.op("gpsimd", hbfc,
                      waits=[("dv", pr.n("dv")), ("pe", wave_pe)], inc="po")
            hbf_po = pr.n("po")
            sts_dv = pr.n("dv")

            if not main:
                st["Xtok_rd"][par] = [("pe", pr.n("pe")), ("dv", sts_dv)]

            if c == NCW:
                self.dump("c4_Gsb", G_sb[:])
                self.dump("c4_Wt8", Wt8[:])
                self.dump("c4_G8", G8[:])
                self.dump("c4_sm2", sm2[:])
                self.dump("c4_y", y[:])
                self.dump("c4_X", Xtok[par][:])
                self.dump("c4_hbf", h_bf[:])

            if c == NCW - 1:
                pr.op("vector", lambda e: e.tensor_scalar_mul(
                    h_sb[:], h_sb[:], self.mv[:, 0:1]),
                    waits=[("dv", sts_dv), ("dA", pr.n("dA"))], inc="dv")
                pr.op("vector", lambda e: e.scalar_tensor_tensor(
                    h_sb[:], h0sb[:], self.mv[:, 1:2], h_sb[:], OP.mult, OP.add),
                    waits=[("dv", pr.n("dv"))], inc="dv")
                pr.op("gpsimd", lambda e: e.tensor_copy(h_bf[:], h_sb[:]),
                      waits=[("dv", pr.n("dv")), ("pe", pr.n("pe"))], inc="po")

            if main and MLVL >= 7:
                pr.op("scalar", lambda e, par=par: e.activation(
                    Xtok[par][:], y[:], AF.Square, accum_out=ssqy[:]),
                    waits=[("dv", sts_dv)], inc="ac")
                st["y_ac"] = pr.n("ac")
                st["Xtok_rd"][par] = [("ac", pr.n("ac")), ("dv", sts_dv),
                                      ("pe", pr.n("pe"))]
                pr.op("scalar", lambda e: e.activation(
                    yt1[:], ssqy[:], AF.Ln, bias=self.cc[:, 0:1], scale=1.0 / DIN),
                    waits=[("ac", pr.n("ac"))], inc="ac")
                pr.op("scalar", lambda e: e.activation(
                    yt1[:], yt1[:], AF.Exp, scale=-0.5),
                    waits=[("ac", pr.n("ac"))], inc="ac")
                rstd_ac = pr.n("ac")
                pr.op("vector", lambda e, zpar=zpar: e.scalar_tensor_tensor(
                    zld[zpar][:], y[:], yt1[:], zld[zpar][:], OP.mult, OP.mult),
                    waits=[("ac", max(rstd_ac, siluz_ac))], inc="dv")
                yf_dv = pr.n("dv")

                for w in range(3):
                    bx = w % 2

                    def ytp(e, zpar=zpar, w=w, bx=bx):
                        last = None
                        for i in range(8):
                            j = w * 8 + i
                            last = e.matmul(bX[bx][:, i * P:(i + 1) * P],
                                            zld[zpar][:, j * P:(j + 1) * P],
                                            self.identb[:], is_transpose=True,
                                            start=(i == 0), stop=(i == 7))
                        return last
                    pr.op("tensor", ytp,
                          waits=[("dv", yf_dv), ("ac", st["psX_ac"][bx])], inc="pe")
                    pe_h = pr.n("pe")

                    def yev(e, par=par, w=w, bx=bx):
                        last = None
                        for i in range(8):
                            j = w * 8 + i
                            last = e.copy(Xsil[par][:, j, :],
                                          bX[bx][:, i * P:(i + 1) * P])
                        return last
                    pr.op("scalar", yev, waits=[("pe", pe_h)], inc="ac")
                    st["psX_ac"][bx] = pr.n("ac")
                st["zld_rd"][zpar] = [("pe", pr.n("pe"))]
                ytp_pe = pr.n("pe")
                pr.op("scalar", lambda e, par=par, cm=cm: e.dma_start(
                    out=self.ygd.rearrange("j p t -> p j t")[:, :, cm * P:(cm + 1) * P],
                    in_=Xsil[par][:]),
                    waits=[("ac", pr.n("ac"))], inc="dS", inck=16)
                st["Xsil_rd"][par] = [("pe", ytp_pe), ("dS", pr.n("dS"))]

        self.dump("h_final", h_sb[:])
        self.dump("y_last", y[:])
        self.dump("ygd", self.ygd)

    def phase_p4(self):
        pr, nc = self.pr, self.nc
        pb = [b[:] for b in self.pbank]
        p2_done = [(s, pr.n(s)) for s in list(pr.semn)]

        wdf = [self.man(f"wdf{i}", [P, 512], F32, i * 2048) for i in range(2)]
        wdb = [self.man(f"wdb{i}", [P, 512], BF16, 4096 + i * 1024) for i in range(2)]
        atile = [self.man(f"atile{i}", [P, 1024], BF16, 6144 + i * 2048) for i in range(2)]
        g_sb = self.man("g_sb4", [P, 8, 512], BF16, 10 * 1024)
        xld4 = [self.man(f"xld4_{i}", [P, 512], F32, 18 * 1024 + i * 2048) for i in range(2)]
        xnb = self.man("xnb4", [P, 512], F32, 22 * 1024)
        outb = [self.man(f"outb{i}", [P, 512], F32, 24 * 1024 + i * 2048) for i in range(2)]
        ygt = [self.man(f"ygt{i}", [P, 1024], BF16, 28 * 1024 + i * 2048) for i in range(2)]

        wld = [p2_done, p2_done]
        wcast = [0, 0]
        ald = [p2_done, p2_done]
        bank_dv = p2_done
        g_po = 0
        out_ds = [0, 0]
        xldw = [p2_done, p2_done]

        NF = FDOWN // P
        NO = DIN // P

        for tg in range(2):
            for do in range(4):
                cast_engs = ("gpsimd", "vector", "scalar")
                cast_sems = ("po", "dv", "ac")
                for f in range(NF):
                    par = f % 2
                    pr.op("sync", lambda e, f=f, do=do, par=par: e.dma_start(
                              out=wdf[par][:],
                              in_=self.wd_h[f * P:(f + 1) * P, do * 512:(do + 1) * 512]),
                          waits=wld[par], inc="dW", inck=16)
                    dw = [("dW", pr.n("dW"))]
                    ce, cs = cast_engs[f % 3], cast_sems[f % 3]

                    def castop(e, par=par):
                        if hasattr(e, "tensor_copy"):
                            return e.tensor_copy(wdb[par][:], wdf[par][:])
                        return e.copy(wdb[par][:], wdf[par][:])
                    pr.op(ce, castop, waits=dw + [("pe", wcast[par])], inc=cs)
                    wld[par] = [(cs, pr.n(cs))]
                    cast_r = [(cs, pr.n(cs))]
                    pr.op("scalar", lambda e, f=f, tg=tg, par=par: e.dma_start(
                        out=atile[par][:],
                        in_=self.ad[f * P:(f + 1) * P, tg * 1024:(tg + 1) * 1024]),
                        waits=ald[par], inc="dA", inck=16)
                    da = pr.n("dA")

                    def dmm(e, f=f, par=par):
                        last = None
                        for tc in range(8):
                            last = e.matmul(pb[tc][:],
                                            atile[par][:, tc * P:(tc + 1) * P],
                                            wdb[par][:],
                                            start=(f == 0), stop=(f == NF - 1))
                        return last
                    pr.op("tensor", dmm,
                          waits=cast_r + [("dA", da)] +
                          (bank_dv if f == 0 else []), inc="pe")
                    wcast[par] = pr.n("pe")
                    ald[par] = [("pe", pr.n("pe"))]
                down_pe = pr.n("pe")

                for tc in range(8):
                    pr.op("scalar", lambda e, tc=tc: e.activation(
                        g_sb[:, tc, :], pb[tc][:], AF.Silu),
                        waits=[("pe", down_pe), ("dv", g_po)], inc="ac")
                silu_ac = pr.n("ac")

                for fo in range(NO):
                    par = fo % 2
                    pr.op("sync", lambda e, fo=fo, do=do, par=par: e.dma_start(
                        out=wdf[par][:],
                        in_=self.wo_h[fo * P:(fo + 1) * P, do * 512:(do + 1) * 512]),
                        waits=wld[par], inc="dW", inck=16)
                    dw = pr.n("dW")
                    ce, cs = ("gpsimd", "po") if fo % 2 == 0 else ("vector", "dv")
                    pr.op(ce, lambda e, par=par, fo=fo: e.tensor_scalar_mul(
                        wdb[par][:], wdf[par][:], self.innerw[:, fo:fo + 1]),
                        waits=[("dW", dw), ("pe", wcast[par])], inc=cs)
                    wld[par] = [(cs, pr.n(cs))]
                    cast_r = [(cs, pr.n(cs))]
                    pr.op("scalar", lambda e, fo=fo, tg=tg, par=par: e.dma_start(
                        out=ygt[par][:],
                        in_=self.ygd[fo, :, tg * 1024:(tg + 1) * 1024]),
                        waits=ald[par], inc="dA", inck=16)
                    da = pr.n("dA")

                    def omm(e, fo=fo, par=par):
                        last = None
                        for tc in range(8):
                            last = e.matmul(pb[tc][:],
                                            ygt[par][:, tc * P:(tc + 1) * P],
                                            wdb[par][:],
                                            start=(fo == 0), stop=(fo == NO - 1))
                        return last
                    pr.op("tensor", omm,
                          waits=cast_r + [("dA", da)] +
                          ([("ac", silu_ac)] if fo == 0 else []), inc="pe")
                    wcast[par] = pr.n("pe")
                    ald[par] = [("pe", pr.n("pe"))]
                oproj_pe = pr.n("pe")

                dv_marks = []
                for tc in range(8):
                    xpar = tc % 2
                    c = NCW + tg * 8 + tc
                    r0 = WARM + tg * 1024 + tc * P
                    pr.op("sync", lambda e, r0=r0, do=do, xpar=xpar: e.dma_start(
                        out=xld4[xpar][:],
                        in_=self.x_h[r0:r0 + P, do * 512:(do + 1) * 512]),
                        waits=xldw[xpar], inc="dX", inck=16)
                    dxh = pr.n("dX")

                    def comb(e, tc=tc, c=c, xpar=xpar, do=do):
                        e.tensor_scalar(xnb[:], xld4[xpar][:],
                                        self.mrstd[:, 2 * c:2 * c + 1],
                                        self.mrstd[:, 2 * c + 1:2 * c + 2],
                                        OP.subtract, OP.mult)
                        e.tensor_tensor(xnb[:], xnb[:], pb[tc][:], OP.add)
                        e.tensor_tensor(xnb[:], xnb[:], g_sb[:, tc, :], OP.mult)
                        return e.tensor_tensor(outb[tc % 2][:], xnb[:],
                                               xld4[xpar][:], OP.add)
                    pr.op("vector", comb,
                          waits=[("dX", dxh), ("pe", oproj_pe),
                                 ("dS", out_ds[tc % 2])], inc="dv")
                    xldw[xpar] = [("dv", pr.n("dv"))]
                    dv_marks.append(pr.n("dv"))
                    pr.op("scalar", lambda e, r0=r0, do=do, tc=tc: e.dma_start(
                        out=self.out_h[r0 - WARM:r0 - WARM + P,
                                       do * 512:(do + 1) * 512],
                        in_=outb[tc % 2][:]),
                        waits=[("dv", pr.n("dv"))], inc="dS", inck=16)
                    out_ds[tc % 2] = pr.n("dS")
                bank_dv = [("dv", pr.n("dv"))]
                g_po = pr.n("dv")


def S_g(gsc, g):
    return gsc[:, :, g]


def build_program(dbg=None, phases=("C", "P0", "P1", "P3a", "P2", "P4")):
    k = Kernel(dbg)
    if "C" in phases:
        k.phase_consts()
    if "P0" in phases:
        k.phase_p0()
    if "P1" in phases:
        k.phase_p1()
        k.dump("zd", k.zd)
        k.dump("xcd", k.xcd)
    if "P3a" in phases:
        k.phase_p3a()
    if "P2" in phases:
        k.phase_p2()
    if "P4" in phases:
        k.phase_p4()
    pr = k.pr
    waits = [(s, pr.n(s)) for s in list(pr.semn)]
    pr.op("sync", lambda e: e.nop(), waits=waits)

    with k.nc.Block() as blk:
        pr.emit(blk)
    return k


def kernel(**inputs):
    maps = prep_inputs(inputs)
    k = build_program()
    res = run_bass_kernel_spmd(k.nc, maps, core_ids=list(range(8)))
    out = np.zeros((2, 8192, 2048), np.float32)
    for core in range(8):
        b, q = core // 4, core % 4
        out[b, q * TMAIN:(q + 1) * TMAIN] = res.results[core]["out"]
    return out


def prep_inputs(inputs):
    x = np.ascontiguousarray(inputs["L0_attn_x"])
    w1t = np.ascontiguousarray(np.asarray(inputs["L0_attn_in_proj_w"]).T)
    gate_w = np.asarray(inputs["L0_ffn_gate_w"])
    up_w = np.asarray(inputs["L0_ffn_up_w"])
    parts = []
    for n in range(NE):
        parts.append(gate_w[n].T)
        parts.append(up_w[n].T)
    parts.append(np.asarray(inputs["L0_ffn_sh_gate_w"]).T)
    parts.append(np.asarray(inputs["L0_ffn_sh_up_w"]).T)
    w3t = np.ascontiguousarray(np.concatenate(parts, axis=1))
    dparts = [np.ascontiguousarray(np.asarray(inputs["L0_ffn_down_w"])[n].T) for n in range(NE)]
    dparts.append(np.ascontiguousarray(np.asarray(inputs["L0_ffn_sh_down_w"]).T))
    wdt = np.ascontiguousarray(np.concatenate(dparts, axis=0))
    wot = np.ascontiguousarray(np.asarray(inputs["L0_attn_out_proj_w"]).T)
    gwt = np.ascontiguousarray(np.asarray(inputs["L0_ffn_gate_weight"]).T)
    normw_kp = np.ascontiguousarray(np.asarray(inputs["L0_attn_norm_w"]).reshape(KD, P).T)
    innerw_kp = np.ascontiguousarray(np.asarray(inputs["L0_attn_inner_norm_w"]).reshape(JIN, P).T)
    convw_fm = np.ascontiguousarray(
        np.asarray(inputs["L0_attn_conv_w"]).reshape(JIN, P, 4).transpose(1, 0, 2))
    convb_fm = np.ascontiguousarray(np.asarray(inputs["L0_attn_conv_b"]).reshape(JIN, P).T)
    cs_fm = np.ascontiguousarray(
        np.asarray(inputs["L0_attn_conv_state"])[0].reshape(JIN, P, 3).transpose(1, 0, 2))
    h0_fm = np.ascontiguousarray(
        np.asarray(inputs["L0_attn_ssm_state"])[0].transpose(1, 0, 2).reshape(DS, NH * DH))
    abd = np.zeros((NH, 4), np.float32)
    abd[:, 0] = np.asarray(inputs["L0_attn_A_log"])
    abd[:, 1] = np.asarray(inputs["L0_attn_dt_bias"])
    abd[:, 2] = np.asarray(inputs["L0_attn_D_param"])

    shared = dict(w1t=w1t, w3t=w3t, wdt=wdt, wot=wot, gwt=gwt,
                  normw_kp=normw_kp, innerw_kp=innerw_kp, convw_fm=convw_fm,
                  convb_fm=convb_fm, cs_fm=cs_fm, h0_fm=h0_fm, abd=abd)
    maps = []
    for core in range(8):
        b, q = core // 4, core % 4
        s0 = q * TMAIN
        if q == 0:
            x_ext = np.concatenate([np.zeros((WARM, D), np.float32), x[b, :TMAIN]], axis=0)
            mval = 0.0
        else:
            x_ext = np.ascontiguousarray(x[b, s0 - WARM:s0 + TMAIN])
            mval = 1.0
        mvec = np.empty((P, 2), np.float32)
        mvec[:, 0] = mval
        mvec[:, 1] = 1.0 - mval
        m = dict(shared)
        m["x_ext"] = np.ascontiguousarray(x_ext, dtype=np.float32)
        m["mvec"] = mvec
        maps.append(m)
    return maps
